# version 18
# baseline (speedup 1.0000x reference)
import contextlib

import numpy as np

import concourse.bass as bass
import concourse.mybir as mybir
from concourse.bass_utils import run_bass_kernel_spmd

F32 = mybir.dt.float32
BF16 = mybir.dt.bfloat16
AF = mybir.ActivationFunctionType
ALU = mybir.AluOpType
AX = mybir.AxisListType

T = 2048
D = 2048
NST = 16
E = 2080
FF = 8192
TC = 256
NCH = T // TC
NTT = TC // 128
NTB = T // 128
ND = D // 128
NEB = 17
NFS = FF // 512
NFC = FF // 128
NDQ = 4
EPS = 1e-5
EPS_W = 1e-8

ENGS = ("sp", "act", "dve", "pool", "pe")
STOP_PHASE = None


class Sched:

    def __init__(self):
        self.steps = []
        self.cum = {}
        self.tags = {}
        self.last = {}
        self.pending_fence = None
        self.fence_seen = set()
        self.ntag = 0

    def _push(self, eng, stream, fn, waits, tag, inc):
        waits = list(waits)
        if self.pending_fence is not None and eng not in self.fence_seen:
            waits.append(self.pending_fence)
            self.fence_seen.add(eng)
        for wtag in waits:
            assert wtag in self.tags, f"unknown wait tag {wtag}"
        v = self.cum.get(stream, 0) + inc
        self.cum[stream] = v
        if tag is None:
            tag = f"_s{self.ntag}"
        self.ntag += 1
        assert tag not in self.tags, tag
        self.tags[tag] = (stream, v)
        self.last[stream] = tag
        self.steps.append((eng, stream, fn, tuple(waits), inc))
        return tag

    def comp(self, eng, fn, tag=None, waits=(), n=1):
        return self._push(eng, eng, fn, waits, tag, n)

    def dma(self, eng, stream, fn, tag=None, waits=(), n=1):
        return self._push(eng, "D" + stream, fn, waits, tag, 16 * n)

    def opt(self, *tags):
        return [t for t in tags if t in self.tags]

    def barrier(self, name):
        waits = list(self.last.values())
        self.comp("pool", lambda e: e.engine_nop(), tag=name, waits=waits)
        self.pending_fence = name
        self.fence_seen = {"pool"}

    def emit(self, nc, stack):
        streams = sorted({s for (_, s, _, _, _) in self.steps})
        sems = {s: stack.enter_context(nc.semaphore(f"sm_{i}"))
                for i, s in enumerate(streams)}
        per_eng = {e: [] for e in ENGS}
        for st in self.steps:
            per_eng[st[0]].append(st)
        watermark = {}

        def runner(eng_name):
            def run(eng_obj):
                for (eng, stream, fn, waits, inc) in per_eng[eng_name]:
                    for wtag in waits:
                        ws, wv = self.tags[wtag]
                        key = (eng_name, ws)
                        if watermark.get(key, -1) < wv:
                            eng_obj.wait_ge(sems[ws], wv)
                            watermark[key] = wv
                    insts = fn(eng_obj)
                    if insts is None:
                        insts = []
                    if not isinstance(insts, (list, tuple)):
                        insts = [insts]
                    per = 16 if stream.startswith("D") else 1
                    assert len(insts) * per == inc, (stream, len(insts), inc)
                    for i in insts:
                        i.then_inc(sems[stream], per)
            return run

        with nc.Block() as block:
            block.sync(runner("sp"))
            block.scalar(runner("act"))
            block.vector(runner("dve"))
            block.gpsimd(runner("pool"))
            block.tensor(runner("pe"))


def build_nc():
    nc = bass.Bass("TRN2", target_bir_lowering=False)
    S = Sched()
    opt = S.opt
    rsx = {"last": None}

    x_h = nc.dram_tensor("x", [T, D], F32, kind="ExternalInput").ap()
    wxp_h = nc.dram_tensor("wxp", [E, D], F32, kind="ExternalInput").ap()
    app_h = nc.dram_tensor("app", [128, ND * NST], F32, kind="ExternalInput").ap()
    dpp_h = nc.dram_tensor("dpp", [128, ND], F32, kind="ExternalInput").ap()
    w1_h = nc.dram_tensor("w1", [FF, D], F32, kind="ExternalInput").ap()
    b1_h = nc.dram_tensor("b1r", [1, FF], F32, kind="ExternalInput").ap()
    w2_h = nc.dram_tensor("w2", [D, FF], F32, kind="ExternalInput").ap()
    b2_h = nc.dram_tensor("b2r", [1, D], F32, kind="ExternalInput").ap()
    ident_h = nc.dram_tensor("ident", [128, 128], F32, kind="ExternalInput").ap()
    identb_h = nc.dram_tensor("identb", [128, 128], BF16, kind="ExternalInput").ap()
    ones_h = nc.dram_tensor("ones", [128, 512], F32, kind="ExternalInput").ap()
    repl_h = nc.dram_tensor("repl", [32, 32 * 128], F32, kind="ExternalInput").ap()
    out_h = nc.dram_tensor("out", [T, D], F32, kind="ExternalOutput").ap()

    wxpt_h = nc.dram_tensor("wxpt_s", [NEB, 128, ND * 128], F32, kind="Internal").ap()
    wq1t_h = nc.dram_tensor("wq1t_s", [NFS, 128, ND * 512], BF16, kind="Internal").ap()
    wq2t_h = nc.dram_tensor("wq2t_s", [16, 128, NFC * 128], BF16, kind="Internal").ap()
    r1_h = nc.dram_tensor("r1_s", [NTB, 128, D], F32, kind="Internal").ap()
    g_h = nc.dram_tensor("g_s", [NTB, 128, FF], F32, kind="Internal").ap()
    xq2t_h = nc.dram_tensor("xq2t_s", [NTB, 128, NFC * 128], BF16, kind="Internal").ap()
    y2_h = nc.dram_tensor("y2_s", [NTB, 128, D], F32, kind="Internal").ap()

    stack = contextlib.ExitStack()

    def sb(st, name, p, c, dt=F32):
        return st.enter_context(nc.sbuf_tensor(name, [p, c], dt))[:, :]

    ident = sb(stack, "identT", 128, 128)
    identb = sb(stack, "identbT", 128, 128, BF16)
    ones = sb(stack, "onesT", 128, 512)
    app = sb(stack, "appT", 128, ND * NST)
    aneg = sb(stack, "anegT", 128, ND * NST)
    dpp = sb(stack, "dppT", 128, ND)
    q1_a = sb(stack, "q1_a", 128, NTB)
    q1_b = sb(stack, "q1_b", 128, NTB)
    dq1 = sb(stack, "dq1", 128, NTB)
    iq1 = sb(stack, "iq1", 128, NTB)
    q2_a = sb(stack, "q2_a", 128, NTB)
    q2_b = sb(stack, "q2_b", 128, NTB)
    dq2 = sb(stack, "dq2", 128, NTB)
    iq2 = sb(stack, "iq2", 128, NTB)
    bnst = sb(stack, "bnst", 128, 24)
    bnag = sb(stack, "bnag", 128, 2)
    sst = sb(stack, "sst", 128, 8)
    rtmp = sb(stack, "rtmp", 128, 2)
    diagb = sb(stack, "diagb", 128, 128)
    epsc = sb(stack, "epsc", 128, 1)
    halfn = sb(stack, "halfn", 128, 1)
    magp = sb(stack, "magp", 128, 1)
    magn = sb(stack, "magn", 128, 1)
    wsum = sb(stack, "wsum", 128, 96)
    ws1v = sb(stack, "ws1v", 128, 2)
    ws2v = sb(stack, "ws2v", 128, 2)
    wsS = sb(stack, "wsS", 1, 8)
    zmx = sb(stack, "zmx", 128, NTB)
    zmn = sb(stack, "zmn", 128, NTB)

    pb = [stack.enter_context(nc.psum_tensor(f"pb{i}", [128, 512], F32))[:, :]
          for i in range(6)]
    pbf = [stack.enter_context(nc.psum_tensor(f"pbf{i}", [128, 1024], BF16))[:, :]
           for i in range(2)]

    S.comp("dve", lambda e: e.memset(epsc, EPS), tag="epsc")
    S.comp("dve", lambda e: e.memset(magp, 6755399441055744.0), tag="magp")
    S.comp("dve", lambda e: e.memset(magn, -6755399441055744.0), tag="magn")
    S.comp("dve", lambda e: e.memset(halfn, -0.5), tag="halfn")

    def probe_finish(buf):
        S.barrier("probe_bar")
        S.dma("pool", "outs", lambda e: e.dma_start(
            out=out_h[0:128, 0:128], in_=buf), tag="probe_st",
            waits=["probe_bar"])
        S.comp("pool", lambda e: e.engine_nop(), waits=["probe_st"])
        S.emit(nc, stack)
        return nc

    def rstd(ag_tag, out_ap, tag=None):
        w = [ag_tag, "epsc"] + ([rsx["last"]] if rsx["last"] else [])
        t1 = S.comp("act", lambda e: e.activation(
            out=rtmp[:, 0:1], in_=bnag[:, 1:2], func=AF.Ln,
            bias=epsc[:, 0:1]), waits=w)
        t2 = S.comp("act", lambda e, o=out_ap: e.activation(
            out=o, in_=rtmp[:, 0:1], func=AF.Exp, scale=halfn[:, 0:1]),
            waits=[t1, "halfn"], tag=tag)
        rsx["last"] = t2
        return t2

    def agw():
        return [rsx["last"]] if rsx["last"] else []

    for dst, src, nm in ((ident, ident_h, "ident"), (identb, identb_h, "identb"),
                         (ones, ones_h, "ones"), (app, app_h, "app"),
                         (dpp, dpp_h, "dpp")):
        S.dma("sp", "c" + nm, lambda e, d=dst, s=src: e.dma_start(
            out=d, in_=s[:, :]), tag="ld_" + nm)

    t_e = S.comp("act", lambda e: e.activation(out=aneg, in_=app, func=AF.Exp),
                 waits=["ld_app"])
    S.comp("dve", lambda e: e.tensor_scalar_mul(aneg, aneg, -1.0),
           waits=[t_e], tag="aneg")

    W0 = contextlib.ExitStack()
    wxs = sb(W0, "wxs", 128, 2 * D)
    wxo = sb(W0, "wxo", 128, ND * 128)

    for eb in range(NEB):
        sl = eb % 2
        rows = 128 if eb < 16 else 32
        lt = S.dma("sp", f"wxs{sl}", lambda e, eb=eb, sl=sl, rows=rows:
                   e.dma_start(out=wxs[0:rows, sl * D:sl * D + D],
                               in_=wxp_h[eb * 128:eb * 128 + rows, :]),
                   waits=opt(f"wxt{eb-2}q3"), tag=f"wxl{eb}")
        evs = []
        for q in range(4):
            bank = pb[4 + (q % 2)]
            tt = S.comp("pe", lambda e, sl=sl, rows=rows, q=q, bank=bank: [
                e.matmul(bank[0:128, k * 128:k * 128 + rows],
                         wxs[0:rows, sl * D + (q * 4 + k) * 128:
                             sl * D + (q * 4 + k + 1) * 128],
                         ident[0:rows, 0:rows],
                         start=True, stop=True, is_transpose=True)
                for k in range(4)],
                waits=[lt, "ld_ident"] + (opt(f"wxe{eb}q{q-2}") if q >= 2 else
                              opt(f"wxe{eb-1}q{q+2}")),
                n=4, tag=f"wxt{eb}q{q}")
            ev = S.comp("dve", lambda e, q=q, bank=bank: e.tensor_copy(
                wxo[:, q * 512:(q + 1) * 512], bank[:, 0:512]),
                waits=[tt] + opt(f"wxst{eb-1}"), tag=f"wxe{eb}q{q}")
            evs.append(ev)
        S.dma("pool", "wxpt", lambda e, eb=eb: e.dma_start(
            out=wxpt_h[eb, :, :], in_=wxo), waits=evs, tag=f"wxst{eb}")

    W1 = contextlib.ExitStack()
    ws = sb(W1, "wsl", 128, 2 * 4096)
    wabs = sb(W1, "wabs", 128, 4096)

    def wmean(nsl, cols, col0, pfx, src_fn, prev2):
        for si in range(nsl):
            sl = si % 2
            lt = S.dma("sp", f"wsl{sl}", lambda e, si=si, sl=sl, cols=cols:
                       e.dma_start(out=ws[:, sl * 4096:sl * 4096 + cols],
                                   in_=src_fn(si)),
                       waits=opt(f"{pfx}a{si-2}") +
                             ([] if si >= 2 else opt(*prev2(si))),
                       tag=f"{pfx}l{si}")
            S.comp("act", lambda e, si=si, sl=sl, cols=cols, col0=col0:
                   e.activation(out=wabs[:, 0:cols],
                                in_=ws[:, sl * 4096:sl * 4096 + cols],
                                func=AF.Abs,
                                accum_out=wsum[:, col0 + si:col0 + si + 1]),
                   waits=[lt] + opt(f"{pfx}a{si-1}") +
                         ([] if si >= 1 else opt(*prev2(-1))),
                   tag=f"{pfx}a{si}")
        return f"{pfx}a{nsl-1}"

    tm1 = wmean(64, D, 0, "m1_",
                lambda si: w1_h[si * 128:(si + 1) * 128, :], lambda si: ())
    tm2 = wmean(32, 4096, 64, "m2_",
                lambda si: w2_h[(si // 2) * 128:(si // 2 + 1) * 128,
                                (si % 2) * 4096:(si % 2 + 1) * 4096],
                lambda si: (f"m1_a{62+si}" if si >= 0 else "m1_a63",))

    def wfin(col0, nsl, wsv, idx, wm):
        rt = S.comp("dve", lambda e, col0=col0, nsl=nsl, idx=idx: e.reduce_sum(
            sst[:, idx:idx + 1], wsum[:, col0:col0 + nsl], AX.X),
            waits=[wm], tag=f"wred{idx}")
        mt = S.comp("pe", lambda e, idx=idx: e.matmul(
            pb[5][0:1, idx:idx + 1], sst[:, idx:idx + 1], ones[:, 0:1],
            start=True, stop=True),
            waits=[rt, "ld_ones"] + opt("wbc0", f"wxt{NEB-1}q3"),
            tag=f"wps{idx}")
        et = S.comp("dve", lambda e, idx=idx: e.tensor_scalar(
            out=wsS[0:1, idx * 2:idx * 2 + 1], in0=pb[5][0:1, idx:idx + 1],
            scalar1=1.0 / (1 << 24), scalar2=EPS_W, op0=ALU.mult, op1=ALU.max),
            waits=[mt], tag=f"wsc{idx}")
        it = S.comp("dve", lambda e, idx=idx: e.reciprocal(
            wsS[0:1, idx * 2 + 1:idx * 2 + 2], wsS[0:1, idx * 2:idx * 2 + 1]),
            waits=[et], tag=f"winv{idx}")
        bt = S.comp("pe", lambda e, idx=idx: e.matmul(
            pb[5][0:128, 4 + idx:5 + idx], ones[0:1, 0:128],
            wsS[0:1, idx * 2 + 1:idx * 2 + 2], start=True, stop=True),
            waits=[it], tag=f"wbc{idx}")
        ct = S.comp("dve", lambda e, idx=idx, wsv=wsv: e.tensor_copy(
            wsv[:, 0:1], pb[5][:, 4 + idx:5 + idx]), waits=[bt])
        return S.comp("dve", lambda e, wsv=wsv: e.tensor_scalar_mul(
            wsv[:, 1:2], wsv[:, 0:1], -1.0), waits=[ct], tag=f"wneg{idx}")

    tw1 = wfin(0, 64, ws1v, 0, tm1)
    tw2 = wfin(64, 32, ws2v, 1, tm2)

    wqb = sb(W1, "wqb", 128, 4096, BF16)
    wc1 = sb(W1, "wc1", 128, 4096)
    stg1 = sb(W1, "stg1", 128, ND * 512, BF16)
    stg2 = sb(W1, "stg2", 128, NFC * 128, BF16)

    for fs in range(NFS):
        evs_all = []
        for j in range(4):
            si = fs * 4 + j
            sl = si % 2
            lt = S.dma("sp", f"wsl{sl}", lambda e, si=si, sl=sl: e.dma_start(
                out=ws[:, sl * 4096:sl * 4096 + D],
                in_=w1_h[si * 128:(si + 1) * 128, :]),
                waits=opt(f"w1q2_{si-2}") +
                      ([] if si >= 2 else opt(f"m2_a{30+si}")),
                tag=f"w1l{si}")
            c1 = S.comp("dve", lambda e, sl=sl: e.tensor_scalar(
                out=wc1[:, 0:D], in0=ws[:, sl * 4096:sl * 4096 + D],
                scalar1=ws1v[:, 0:1], scalar2=0.5, op0=ALU.mult, op1=ALU.is_ge),
                waits=[lt, tw1] + opt(f"w1qm_{si-1}"), tag=f"w1q1_{si}")
            c2 = S.comp("dve", lambda e, sl=sl: e.tensor_scalar(
                out=wabs[:, 0:D], in0=ws[:, sl * 4096:sl * 4096 + D],
                scalar1=ws1v[:, 1:2], scalar2=0.5, op0=ALU.mult, op1=ALU.is_gt),
                waits=[c1] + opt(*(["m2_a31"] if si == 0 else ())),
                tag=f"w1q2_{si}")
            qm = S.comp("dve", lambda e: e.tensor_sub(
                wqb[:, 0:D], wc1[:, 0:D], wabs[:, 0:D]),
                waits=[c2] + opt(f"w1tp_{si-1}_{ND-1}"), tag=f"w1qm_{si}")
            for dc in range(ND):
                bank = pbf[dc % 2]
                tp = S.comp("pe", lambda e, dc=dc, bank=bank: e.matmul(
                    bank[:, 0:128], wqb[:, dc * 128:(dc + 1) * 128],
                    identb[:, :], start=True, stop=True, is_transpose=True),
                    waits=[qm, "ld_identb"] +
                          (opt(f"w1ev_{si}_{dc-2}") if dc >= 2 else
                           opt(f"w1ev_{si-1}_{ND-2+dc}")),
                    tag=f"w1tp_{si}_{dc}")
                ev = S.comp("act", lambda e, dc=dc, j=j, bank=bank: e.activation(
                    out=stg1[:, dc * 512 + j * 128:dc * 512 + j * 128 + 128],
                    in_=bank[:, 0:128], func=AF.Copy),
                    waits=[tp] + opt(f"w1st_{fs-1}" if j == 0 and dc == 0 else "~"),
                    tag=f"w1ev_{si}_{dc}")
                if j == 3:
                    evs_all.append(ev)
        S.dma("pool", "wq1", lambda e, fs=fs: e.dma_start(
            out=wq1t_h[fs, :, :], in_=stg1),
            waits=evs_all + [f"w1ev_{fs*4+j}_{ND-1}" for j in range(3)],
            tag=f"w1st_{fs}")

    for si in range(16):
        evs_all = []
        for h in range(2):
            s2 = si * 2 + h
            sl = s2 % 2
            lt = S.dma("sp", f"wsl{sl}", lambda e, si=si, h=h, sl=sl: e.dma_start(
                out=ws[:, sl * 4096:(sl + 1) * 4096],
                in_=w2_h[si * 128:(si + 1) * 128, h * 4096:(h + 1) * 4096]),
                waits=opt(f"w2q2_{s2-2}") +
                      ([] if s2 >= 2 else opt(f"w1q2_{62+s2}")),
                tag=f"w2l{s2}")
            c1 = S.comp("dve", lambda e, sl=sl: e.tensor_scalar(
                out=wc1[:, :], in0=ws[:, sl * 4096:(sl + 1) * 4096],
                scalar1=ws2v[:, 0:1], scalar2=0.5, op0=ALU.mult, op1=ALU.is_ge),
                waits=[lt, tw2] + opt(f"w2qm_{s2-1}" if s2 >= 1 else "w1qm_63"),
                tag=f"w2q1_{s2}")
            c2 = S.comp("dve", lambda e, sl=sl: e.tensor_scalar(
                out=wabs[:, :], in0=ws[:, sl * 4096:(sl + 1) * 4096],
                scalar1=ws2v[:, 1:2], scalar2=0.5, op0=ALU.mult, op1=ALU.is_gt),
                waits=[c1], tag=f"w2q2_{s2}")
            qm = S.comp("dve", lambda e: e.tensor_sub(
                wqb[:, :], wc1[:, :], wabs[:, :]),
                waits=[c2] + opt(f"w2tp_{s2-1}_31" if s2 >= 1 else
                                 f"w1tp_63_{ND-1}"), tag=f"w2qm_{s2}")
            for fl in range(32):
                fc = h * 32 + fl
                bank = pbf[fl % 2]
                tp = S.comp("pe", lambda e, fl=fl, bank=bank: e.matmul(
                    bank[:, 0:128], wqb[:, fl * 128:(fl + 1) * 128],
                    identb[:, :], start=True, stop=True, is_transpose=True),
                    waits=[qm] + (opt(f"w2ev_{s2}_{fl-2}") if fl >= 2 else
                                  opt(f"w2ev_{s2-1}_{30+fl}",
                                      f"w1ev_63_{ND-2+fl}")),
                    tag=f"w2tp_{s2}_{fl}")
                ev = S.comp("act", lambda e, fc=fc, bank=bank: e.activation(
                    out=stg2[:, fc * 128:(fc + 1) * 128], in_=bank[:, 0:128],
                    func=AF.Copy),
                    waits=[tp] + opt(f"w2st_{si-1}" if fl == 0 and h == 0 else "~"),
                    tag=f"w2ev_{s2}_{fl}")
                if fl % 8 == 7:
                    evs_all.append(ev)
        S.dma("pool", "wq2", lambda e, si=si: e.dma_start(
            out=wq2t_h[si, :, :], in_=stg2), waits=evs_all, tag=f"w2st_{si}")

    S.barrier("bar_w")
    if STOP_PHASE == "bar_w":
        return probe_finish(ident[0:128, 0:128])
    W1.close()
    W0.close()

    M = contextlib.ExitStack()
    repl = sb(M, "replT", 32, 32 * 128)
    xc = sb(M, "xc", 128, NTT * D)
    xlt = sb(M, "xlt", 128, ND * TC)
    dtb = sb(M, "dtb", 128, ND * TC)
    dxt = sb(M, "dxt", 128, ND * TC)
    bbf = sb(M, "bbf", 128, NST * TC)
    cbf = sb(M, "cbf", 128, NST * TC)
    hbf = sb(M, "hbf", 128, NST * TC)
    aub = sb(M, "aub", 128, 8 * TC)
    spt = sb(M, "spt", 128, 2 * TC)
    ybf = sb(M, "ybf", 128, NTT * D)
    r1b = sb(M, "r1b", 128, D)
    bcb = sb(M, "bcb", 32, TC)
    hst = sb(M, "hst", 128, ND * NST)
    dxx = sb(M, "dxx", 128, 2 * TC)
    hcb = sb(M, "hcb", 128, NST * TC)
    wxt = sb(M, "wxt", 128, 2 * ND * 128)

    S.dma("sp", "crepl", lambda e: e.dma_start(out=repl, in_=repl_h[:, :]),
          tag="ld_repl")
    S.comp("dve", lambda e: e.memset(hst, 0.0), tag="h0", waits=["bar_w"])

    for c in range(NCH):
        for tt in range(NTT):
            tb = c * NTT + tt
            xl = S.dma("sp", f"xch{tt}", lambda e, tb=tb, tt=tt: e.dma_start(
                out=xc[:, tt * D:(tt + 1) * D],
                in_=x_h[tb * 128:(tb + 1) * 128, :]),
                waits=opt(f"xtp{c-1}_{ND-1}"), tag=f"xl{c}_{tt}")
            bs = S.comp("dve", lambda e, tt=tt: [
                e.bn_stats(bnst[:, 6 * i:6 * (i + 1)],
                           xc[:, tt * D + 512 * i:tt * D + 512 * (i + 1)])
                for i in range(4)],
                waits=[xl] + opt(f"ln1ap{c-1}_{tt}", "bar_w"), n=4)
            ag = S.comp("dve", lambda e: e.bn_aggr(bnag, bnst[:, 0:24]),
                        waits=[bs] + agw())
            nm = S.comp("dve", lambda e: e.tensor_scalar_mul(
                sst[:, 0:1], bnag[:, 0:1], -1.0), waits=[ag])
            rs = rstd(ag, sst[:, 1:2])
            S.comp("dve", lambda e, tt=tt: e.tensor_scalar(
                out=xc[:, tt * D:(tt + 1) * D], in0=xc[:, tt * D:(tt + 1) * D],
                scalar1=sst[:, 0:1], scalar2=sst[:, 1:2],
                op0=ALU.add, op1=ALU.mult), waits=[nm, rs], tag=f"ln1ap{c}_{tt}")

        for dc in range(ND):
            bank = pb[4 + (dc % 2)]
            tt_ = S.comp("pe", lambda e, dc=dc, bank=bank: [
                e.matmul(bank[:, tt * 128:(tt + 1) * 128],
                         xc[:, tt * D + dc * 128:tt * D + (dc + 1) * 128],
                         ident[:, :], start=True, stop=True, is_transpose=True)
                for tt in range(NTT)],
                waits=[f"ln1ap{c}_{tt}" for tt in range(NTT)] + ["ld_ident"] +
                      (opt(f"xevq{c}_{dc-2}") if dc >= 2 else
                       opt(f"yev{c-1}_{ND-2+dc}_{NTT-1}")),
                n=NTT, tag=f"xtp{c}_{dc}")
            S.comp("dve", lambda e, dc=dc, bank=bank: e.tensor_copy(
                xlt[:, dc * TC:(dc + 1) * TC], bank[:, 0:TC]),
                waits=[tt_] + opt(f"xpj{c-1}_15", f"dxt{c-1}_{dc}"),
                tag=f"xevq{c}_{dc}")

        def emit_xpj(eb, p):
            erows = 128 if eb < 16 else 32
            sl = p % 2
            if p >= 2:
                pw = opt(f"xpj{c}_{16 if p == 2 else p - 3}")
            else:
                pw = opt(f"xpj{c-1}_{15 - p}", "bar_w")
            lw = S.dma("sp", f"wxt{sl}", lambda e, eb=eb, sl=sl: e.dma_start(
                out=wxt[:, sl * ND * 128:(sl + 1) * ND * 128],
                in_=wxpt_h[eb, :, :]),
                waits=pw, tag=f"wxtl{c}_{eb}")
            bank = pb[p % 2]
            if p >= 3:
                bw = opt(f"spe{c}_{p-3}")
            elif p == 2:
                bw = opt(f"bcc{c}_16")
            else:
                bw = opt(f"spe{c-1}_{15-p}")
            mm = S.comp("pe", lambda e, eb=eb, sl=sl, erows=erows, bank=bank: [
                e.matmul(bank[0:erows, 0:TC],
                         wxt[:, sl * ND * 128 + dc * 128:
                             sl * ND * 128 + dc * 128 + erows],
                         xlt[:, dc * TC:(dc + 1) * TC],
                         start=(dc == 0), stop=(dc == ND - 1))
                for dc in range(ND)],
                waits=[lw] + [f"xevq{c}_{dc}" for dc in range(ND)] + bw,
                n=ND, tag=f"xpj{c}_{eb}")
            if eb < 16:
                e1 = S.comp("act", lambda e, eb=eb, p=p, bank=bank: e.activation(
                    out=spt[:, (p % 2) * TC:(p % 2 + 1) * TC],
                    in_=bank[0:128, 0:TC], func=AF.Exp),
                    waits=[mm] + opt(f"spl{c}_{eb-2}") +
                          ([] if eb >= 2 else opt(f"spl{c-1}_{14+eb}")),
                    tag=f"spe{c}_{eb}")
                S.comp("act", lambda e, eb=eb, p=p: e.activation(
                    out=dtb[:, eb * TC:(eb + 1) * TC],
                    in_=spt[:, (p % 2) * TC:(p % 2 + 1) * TC],
                    func=AF.Ln, bias=1.0),
                    waits=[e1] + opt(f"dxt{c-1}_{eb}",
                                     f"aex{c-1}_{eb*NST+NST-1}"),
                    tag=f"spl{c}_{eb}")
            else:
                S.comp("act", lambda e, bank=bank: e.activation(
                    out=bcb[0:32, 0:TC], in_=bank[0:32, 0:TC], func=AF.Copy),
                    waits=[mm] + opt(f"bcm{c-1}_31"), tag=f"bcc{c}_{eb}")

        emit_xpj(16, 0)

        for k in range(32):
            half, n = divmod(k, 16)
            dstb = bbf if half == 0 else cbf
            bank = pb[2 + (k % 2)]
            bm = S.comp("pe", lambda e, k=k, bank=bank: e.matmul(
                bank[:, 0:TC], repl[0:32, k * 128:(k + 1) * 128],
                bcb[0:32, 0:TC], start=True, stop=True),
                waits=[f"bcc{c}_16", "ld_repl"] +
                      (opt(f"bcv{c}_{k-2}") if k >= 2 else
                       opt(f"bcv{c-1}_{30+k}")),
                tag=f"bcm{c}_{k}")
            wv = [bm]
            if k == 0:
                wv += opt(f"uml{c-1}_255")
            if k == 16:
                wv += opt(f"hcm{c-1}_{ND-1}_{NST-1}")
            S.comp("act", lambda e, n=n, bank=bank, dstb=dstb: e.activation(
                out=dstb[:, n * TC:(n + 1) * TC], in_=bank[:, 0:TC],
                func=AF.Copy), waits=wv, tag=f"bcv{c}_{k}")

        for dh in range(ND):
            hb0 = 0
            emit_xpj(dh, dh + 1)
            S.comp("dve", lambda e, dh=dh: e.tensor_mul(
                dxt[:, dh * TC:(dh + 1) * TC],
                dtb[:, dh * TC:(dh + 1) * TC],
                xlt[:, dh * TC:(dh + 1) * TC]),
                waits=[f"spl{c}_{dh}", f"xevq{c}_{dh}"] +
                      opt(f"uml{c-1}_{dh*NST+NST-1}"), tag=f"dxt{c}_{dh}")
            for n in range(NST):
                g = dh * NST + n
                slot = g % 4
                at = S.comp("act", lambda e, dh=dh, g=g, slot=slot: e.activation(
                    out=aub[:, slot * TC:(slot + 1) * TC],
                    in_=dtb[:, dh * TC:(dh + 1) * TC], func=AF.Exp,
                    scale=aneg[:, g:g + 1]),
                    waits=[f"spl{c}_{dh}", "aneg"] +
                          opt(f"scan{c}_g{g-4}", f"scan{c-1}_g{NST*ND-4+g}"),
                    tag=f"aex{c}_{g}")
                eng_u = "pool" if g % 2 == 1 else "dve"
                ut = S.comp(eng_u, lambda e, dh=dh, n=n, slot=slot: e.tensor_mul(
                    aub[:, (4 + slot) * TC:(5 + slot) * TC],
                    dxt[:, dh * TC:(dh + 1) * TC],
                    bbf[:, n * TC:(n + 1) * TC]),
                    waits=[f"dxt{c}_{dh}", f"bcv{c}_{n}"] +
                          opt(f"scan{c}_g{g-4}", f"scan{c-1}_g{NST*ND-4+g}"),
                    tag=f"uml{c}_{g}")
                sw = [at, ut, "h0"] + opt(
                    f"hcm{c}_{dh-1}_{n}" if dh >= 1 else f"hcm{c-1}_{ND-1}_{n}")
                S.comp("dve", lambda e, g=g, n=n, slot=slot, hb0=hb0:
                       e.tensor_tensor_scan(
                    out=hbf[:, hb0 + n * TC:hb0 + (n + 1) * TC],
                    data0=aub[:, slot * TC:(slot + 1) * TC],
                    data1=aub[:, (4 + slot) * TC:(5 + slot) * TC],
                    initial=hst[:, g:g + 1], op0=ALU.mult, op1=ALU.add),
                    waits=sw, tag=f"scan{c}_g{g}")
            hc = S.comp("dve", lambda e, dh=dh, hb0=hb0: e.tensor_copy(
                hst[:, dh * NST:(dh + 1) * NST],
                hbf[:, hb0 + TC - 1:hb0 + NST * TC:TC]),
                waits=[f"scan{c}_g{dh*NST+NST-1}"], tag=f"hcp{c}_{dh}")
            for n in range(NST):
                eng_h = "pool" if n % 2 == 1 else "dve"
                S.comp(eng_h, lambda e, n=n, hb0=hb0: e.tensor_mul(
                    hcb[:, n * TC:(n + 1) * TC],
                    hbf[:, hb0 + n * TC:hb0 + (n + 1) * TC],
                    cbf[:, n * TC:(n + 1) * TC]),
                    waits=[hc, f"bcv{c}_{16+n}"] +
                          opt(f"ytp{c}_{dh-1}" if dh >= 1 else
                              f"ytp{c-1}_{ND-1}"),
                    tag=f"hcm{c}_{dh}_{n}")
            dxm = S.comp("dve", lambda e, dh=dh: e.tensor_scalar(
                out=dxx[:, (dh % 2) * TC:(dh % 2 + 1) * TC],
                in0=xlt[:, dh * TC:(dh + 1) * TC],
                scalar1=dpp[:, dh:dh + 1], scalar2=None, op0=ALU.mult),
                waits=[f"xevq{c}_{dh}", "ld_dpp"] +
                      opt(f"ytp{c}_{dh-2}" if dh >= 2 else f"ytp{c-1}_{ND-2+dh}"),
                tag=f"dxm{c}_{dh}")
            bank = pb[4 + (dh % 2)]
            yt = S.comp("pe", lambda e, dh=dh, bank=bank, hb0=hb0: [
                mmi
                for tt in range(NTT)
                for mmi in ([
                    e.matmul(bank[:, tt * 128:(tt + 1) * 128],
                             hcb[:, n * TC + tt * 128:
                                 n * TC + (tt + 1) * 128],
                             ident[:, :], start=(n == 0), stop=False,
                             is_transpose=True)
                    for n in range(NST)] + [
                    e.matmul(bank[:, tt * 128:(tt + 1) * 128],
                             dxx[:, (dh % 2) * TC + tt * 128:
                                 (dh % 2) * TC + (tt + 1) * 128],
                             ident[:, :], start=False, stop=True,
                             is_transpose=True)])],
                waits=[f"hcm{c}_{dh}_{n}" for n in range(NST)] + [dxm] +
                      (opt(f"yev{c}_{dh-2}_{NTT-1}") if dh >= 2 else
                       opt(f"xevq{c}_{14+dh}")),
                n=NST * NTT + NTT, tag=f"ytp{c}_{dh}")
            for tt in range(NTT):
                S.comp("act", lambda e, dh=dh, tt=tt, bank=bank: e.activation(
                    out=ybf[:, tt * D + dh * 128:tt * D + (dh + 1) * 128],
                    in_=bank[:, tt * 128:(tt + 1) * 128], func=AF.Copy),
                    waits=[yt] + opt(f"r1af{c-1}_{tt}" if dh == 0 else "~"),
                    tag=f"yev{c}_{dh}_{tt}")

        for tt in range(NTT):
            tb = c * NTT + tt
            bs = S.comp("dve", lambda e, tt=tt: [
                e.bn_stats(bnst[:, 6 * i:6 * (i + 1)],
                           ybf[:, tt * D + 512 * i:tt * D + 512 * (i + 1)])
                for i in range(4)],
                waits=[f"yev{c}_{dh}_{tt}" for dh in range(ND)], n=4)
            ag = S.comp("dve", lambda e: e.bn_aggr(bnag, bnst[:, 0:24]),
                        waits=[bs] + agw())
            nm = S.comp("dve", lambda e: e.tensor_scalar_mul(
                sst[:, 3:4], bnag[:, 0:1], -1.0), waits=[ag])
            rs = rstd(ag, sst[:, 4:5])
            bb_ = S.comp("dve", lambda e: e.tensor_mul(
                sst[:, 5:6], sst[:, 3:4], sst[:, 4:5]), waits=[nm, rs])
            rl = S.dma("sp", "xrl", lambda e, tb=tb: e.dma_start(
                out=r1b, in_=x_h[tb * 128:(tb + 1) * 128, :]),
                waits=opt(f"r1st_{c}_{tt-1}", f"r1st_{c-1}_{NTT-1}"),
                tag=f"xrl{c}_{tt}")
            af0 = S.comp("dve", lambda e, tt=tt: e.tensor_scalar(
                out=ybf[:, tt * D:(tt + 1) * D],
                in0=ybf[:, tt * D:(tt + 1) * D],
                scalar1=sst[:, 4:5], scalar2=sst[:, 5:6],
                op0=ALU.mult, op1=ALU.add),
                waits=[bb_], tag=f"r1aa{c}_{tt}")
            af = S.comp("dve", lambda e, tt=tt: e.tensor_add(
                r1b, ybf[:, tt * D:(tt + 1) * D], r1b),
                waits=[af0, rl], tag=f"r1af{c}_{tt}")
            S.dma("pool", "r1s", lambda e, tb=tb: e.dma_start(
                out=r1_h[tb, :, :], in_=r1b), waits=[af],
                tag=f"r1st_{c}_{tt}")

    S.barrier("bar_m")
    if STOP_PHASE == "bar_m":
        return probe_finish(ident[0:128, 0:128])
    M.close()

    F1 = contextlib.ExitStack()
    GS = contextlib.ExitStack()
    gmx = sb(GS, "gmx", 128, NTB * NFS)
    gmn = sb(GS, "gmn", 128, NTB * NFS)
    gbn = sb(GS, "gbn", 128, NTB * NFS * 6)
    xq1t_st = contextlib.ExitStack()
    xq1t = sb(xq1t_st, "xq1t", 128, ND * T, BF16)
    r1c = sb(F1, "r1c", 128, D)
    zb = sb(F1, "zb", 128, D)
    wb = sb(F1, "wb", 128, D)
    mb_ = sb(F1, "mb", 128, D)
    xqb = sb(F1, "xqb", 128, D, BF16)
    ki1 = sb(F1, "ki1", 128, D, mybir.dt.int32)
    idr1 = sb(F1, "idr1", 1, NTB * 128)

    for tb in range(NTB):
        lt = S.dma("sp", "r1l", lambda e, tb=tb: e.dma_start(
            out=r1c, in_=r1_h[tb, :, :]),
            waits=opt(f"z3_{tb-1}", "bar_m"), tag=f"r1l_{tb}")
        bs = S.comp("dve", lambda e: [
            e.bn_stats(bnst[:, 6 * i:6 * (i + 1)], r1c[:, 512 * i:512 * (i + 1)])
            for i in range(4)], waits=[lt] + opt(f"z3_{tb-1}", "bar_m"), n=4)
        ag = S.comp("dve", lambda e: e.bn_aggr(bnag, bnst[:, 0:24]), waits=[bs] + agw())
        nm = S.comp("dve", lambda e: e.tensor_scalar_mul(
            sst[:, 0:1], bnag[:, 0:1], -1.0), waits=[ag])
        rs = rstd(ag, sst[:, 1:2])
        zt = S.comp("dve", lambda e: e.tensor_scalar(
            out=zb, in0=r1c, scalar1=sst[:, 0:1], scalar2=sst[:, 1:2],
            op0=ALU.add, op1=ALU.mult), waits=[nm, rs], tag=f"z3_{tb}")
        bs4 = S.comp("dve", lambda e: [
            e.bn_stats(bnst[:, 6 * i:6 * (i + 1)], zb[:, 512 * i:512 * (i + 1)])
            for i in range(4)], waits=[zt], n=4)
        ag4 = S.comp("dve", lambda e: e.bn_aggr(bnag, bnst[:, 0:24]),
                     waits=[bs4] + agw())
        mx = S.comp("dve", lambda e, tb=tb: e.tensor_reduce(
            zmx[:, tb:tb + 1], zb, AX.X, ALU.max), waits=[zt])
        mn = S.comp("dve", lambda e, tb=tb: e.tensor_reduce(
            zmn[:, tb:tb + 1], zb, AX.X, ALU.min), waits=[zt])
        nm4 = S.comp("dve", lambda e: e.tensor_scalar_mul(
            sst[:, 0:1], bnag[:, 0:1], -1.0), waits=[ag4])
        rs4 = rstd(ag4, sst[:, 1:2])
        am1 = S.comp("dve", lambda e, tb=tb: e.tensor_add(
            sst[:, 2:3], zmx[:, tb:tb + 1], sst[:, 0:1]), waits=[mx, nm4])
        am2 = S.comp("dve", lambda e, tb=tb: e.scalar_tensor_tensor(
            out=sst[:, 3:4], in0=zmn[:, tb:tb + 1], scalar=-1.0,
            in1=sst[:, 0:1], op0=ALU.mult, op1=ALU.subtract), waits=[mn, nm4])
        am = S.comp("dve", lambda e: e.tensor_max(
            sst[:, 2:3], sst[:, 2:3], sst[:, 3:4]), waits=[am1, am2])
        amr = S.comp("dve", lambda e: e.tensor_scalar(
            out=sst[:, 2:3], in0=sst[:, 2:3], scalar1=sst[:, 1:2], scalar2=EPS,
            op0=ALU.mult, op1=ALU.max), waits=[am, rs4])
        rec = S.comp("dve", lambda e: e.reciprocal(sst[:, 4:5], sst[:, 2:3]),
                     waits=[amr])
        sc = S.comp("dve", lambda e: e.tensor_scalar_mul(
            sst[:, 4:5], sst[:, 4:5], 127.0), waits=[rec])
        a_ = S.comp("dve", lambda e, tb=tb: e.tensor_mul(
            q1_a[:, tb:tb + 1], sst[:, 1:2], sst[:, 4:5]), waits=[sc],
            tag=f"q1a_{tb}")
        b_ = S.comp("dve", lambda e, tb=tb: e.tensor_mul(
            q1_b[:, tb:tb + 1], sst[:, 0:1], q1_a[:, tb:tb + 1]),
            waits=[a_], tag=f"q1b_{tb}")
        iq = S.comp("dve", lambda e, tb=tb: e.tensor_mul(
            iq1[:, tb:tb + 1], sst[:, 4:5], ws1v[:, 0:1]), waits=[sc, tw1],
            tag=f"iq1_{tb}")
        S.comp("dve", lambda e, tb=tb: e.reciprocal(
            dq1[:, tb:tb + 1], iq1[:, tb:tb + 1]), waits=[iq], tag=f"dq1_{tb}")
        dg = S.comp("dve", lambda e, tb=tb: e.tensor_scalar(
            out=diagb, in0=ident, scalar1=iq1[:, tb:tb + 1], scalar2=None,
            op0=ALU.mult),
            waits=[f"iq1_{tb}", "ld_ident"] + opt(f"idr1t_{tb-1}"),
            tag=f"idr1d_{tb}")
        tp0 = S.comp("pe", lambda e, tb=tb: e.matmul(
            pb[5][0:1, (tb % 4) * 128:(tb % 4) * 128 + 128],
            ones[:, 0:1], diagb, start=True, stop=True),
            waits=[dg, "ld_ones"] + opt(f"idr1_{tb-4}"), tag=f"idr1t_{tb}")
        S.comp("dve", lambda e, tb=tb: e.tensor_copy(
            idr1[0:1, tb * 128:(tb + 1) * 128],
            pb[5][0:1, (tb % 4) * 128:(tb % 4) * 128 + 128]), waits=[tp0],
            tag=f"idr1_{tb}")
        w1_ = S.comp("dve", lambda e, tb=tb: e.tensor_scalar(
            out=wb, in0=zb, scalar1=q1_a[:, tb:tb + 1],
            scalar2=q1_b[:, tb:tb + 1], op0=ALU.mult, op1=ALU.add),
            waits=[b_] + opt(f"xq1m_{tb-1}"), tag=f"xq1w_{tb}")
        kcv = S.comp("dve", lambda e: e.tensor_copy(ki1, wb),
                     waits=[w1_], tag=f"xq1k_{tb}")
        dd = S.comp("dve", lambda e: e.tensor_sub(mb_, wb, ki1),
                    waits=[kcv], tag=f"xq1d_{tb}")
        cc = S.comp("dve", lambda e: e.tensor_scalar(
            out=mb_, in0=mb_, scalar1=magp[:, 0:1], scalar2=magn[:, 0:1],
            op0=ALU.add, op1=ALU.add), waits=[dd, "magp", "magn"],
            tag=f"xq1c_{tb}")
        xq = S.comp("dve", lambda e: e.tensor_add(xqb, mb_, ki1),
                    waits=[cc] + opt(f"xq1tp_{tb-1}_{ND-1}"),
                    tag=f"xq1m_{tb}")
        for dc in range(ND):
            bank = pbf[dc % 2]
            tp = S.comp("pe", lambda e, dc=dc, bank=bank: e.matmul(
                bank[:, 0:128], xqb[:, dc * 128:(dc + 1) * 128], identb[:, :],
                start=True, stop=True, is_transpose=True),
                waits=[xq] + (opt(f"xq1ev_{tb}_{dc-2}") if dc >= 2 else
                              opt(f"xq1ev_{tb-1}_{ND-2+dc}", "bar_m")),
                tag=f"xq1tp_{tb}_{dc}")
            S.comp("act", lambda e, dc=dc, tb=tb, bank=bank: e.activation(
                out=xq1t[:, dc * T + tb * 128:dc * T + (tb + 1) * 128],
                in_=bank[:, 0:128], func=AF.Copy), waits=[tp],
                tag=f"xq1ev_{tb}_{dc}")


    F2 = F1
    b1r = sb(F2, "b1rT", 1, FF)
    wq1s = sb(F2, "wq1s", 128, 2 * ND * 512, BF16)
    gb = sb(F2, "gb", 128, 2 * 512)

    S.dma("sp", "cb1", lambda e: e.dma_start(out=b1r, in_=b1_h[:, :]),
          tag="ld_b1", waits=["bar_m"])

    for fs in range(NFS):
        sl = fs % 2
        lw = S.dma("sp", f"wq1l{sl}", lambda e, fs=fs, sl=sl: e.dma_start(
            out=wq1s[:, sl * ND * 512:(sl + 1) * ND * 512],
            in_=wq1t_h[fs, :, :]),
            waits=[f"w1st_{fs}"] + opt(f"mm1_{fs-2}_{NTB-1}"),
            tag=f"wq1l_{fs}")
        for tb in range(NTB):
            bank = pb[tb % 2]
            mm = S.comp("pe", lambda e, fs=fs, sl=sl, tb=tb, bank=bank: [
                e.matmul(bank[:, 0:512],
                         xq1t[:, dc * T + tb * 128:dc * T + (tb + 1) * 128],
                         wq1s[:, sl * ND * 512 + dc * 512:
                              sl * ND * 512 + (dc + 1) * 512],
                         start=(dc == 0), stop=False)
                for dc in range(ND)] + [
                e.matmul(bank[:, 0:512], idr1[0:1, tb * 128:(tb + 1) * 128],
                         b1r[0:1, fs * 512:(fs + 1) * 512],
                         start=False, stop=True)],
                waits=[lw, f"xq1ev_{tb}_{ND-1}", f"idr1_{tb}", "ld_b1"] +
                      opt(f"gev_{fs}_{tb-2}", f"gev_{fs-1}_{NTB+tb-2}"),
                n=ND + 1, tag=f"mm1_{fs}_{tb}")
            gv = gb[:, (tb % 2) * 512:(tb % 2) * 512 + 512]
            ge = S.comp("act", lambda e, tb=tb, bank=bank, gv=gv: e.activation(
                out=gv, in_=bank[:, 0:512], func=AF.Gelu,
                scale=dq1[:, tb:tb + 1]),
                waits=[mm, f"dq1_{tb}"] + opt(f"gst_{fs}_{tb-2}",
                                              f"gst_{fs-1}_{NTB+tb-2}"),
                tag=f"gev_{fs}_{tb}")
            S.comp("dve", lambda e, fs=fs, tb=tb, gv=gv: e.bn_stats(
                gbn[:, (tb * NFS + fs) * 6:(tb * NFS + fs) * 6 + 6], gv),
                waits=[ge], tag=f"gbn_{fs}_{tb}")
            S.comp("dve", lambda e, fs=fs, tb=tb, gv=gv: e.tensor_reduce(
                gmx[:, tb * NFS + fs:tb * NFS + fs + 1], gv, AX.X, ALU.max),
                waits=[ge], tag=f"gmx_{fs}_{tb}")
            gmn_t = S.comp("dve", lambda e, fs=fs, tb=tb, gv=gv: e.tensor_reduce(
                gmn[:, tb * NFS + fs:tb * NFS + fs + 1], gv, AX.X, ALU.min),
                waits=[ge], tag=f"gmn_{fs}_{tb}")
            S.dma("pool", f"gs{tb % 2}", lambda e, fs=fs, tb=tb, gv=gv: e.dma_start(
                out=g_h[tb, :, fs * 512:(fs + 1) * 512], in_=gv),
                waits=[gmn_t], tag=f"gst_{fs}_{tb}")

    S.barrier("bar_f2")
    if STOP_PHASE == "bar_f2":
        return probe_finish(ident[0:128, 0:128])
    F2.close()
    xq1t_st.close()

    ID2 = contextlib.ExitStack()
    idr2 = sb(ID2, "idr2", 1, NTB * 128)
    F3 = contextlib.ExitStack()
    QW = FF // 4
    gr = sb(F3, "gr", 128, QW)
    wb3 = sb(F3, "wb3", 128, QW)
    mb3 = sb(F3, "mb3", 128, QW)
    xq2b = sb(F3, "xq2b", 128, QW, BF16)
    ki3 = sb(F3, "ki3", 128, QW, mybir.dt.int32)
    stg3 = sb(F3, "stg3", 128, NFC * 128, BF16)
    gmx3 = gmx
    gmn3 = gmn
    gbn3 = gbn

    for tb in range(NTB):
        ag = S.comp("dve", lambda e, tb=tb: e.bn_aggr(
            bnag, gbn3[:, tb * NFS * 6:(tb + 1) * NFS * 6]),
            waits=opt(f"xq2w_{tb-1}_3", "bar_f2") + agw(), tag=f"gag_{tb}")
        nm = S.comp("dve", lambda e: e.tensor_scalar_mul(
            sst[:, 0:1], bnag[:, 0:1], -1.0), waits=[ag])
        rs = rstd(ag, sst[:, 1:2])
        mx = S.comp("dve", lambda e, tb=tb: e.tensor_reduce(
            sst[:, 2:3], gmx3[:, tb * NFS:(tb + 1) * NFS], AX.X, ALU.max),
            waits=[ag])
        mn = S.comp("dve", lambda e, tb=tb: e.tensor_reduce(
            sst[:, 3:4], gmn3[:, tb * NFS:(tb + 1) * NFS], AX.X, ALU.min),
            waits=[ag])
        am1 = S.comp("dve", lambda e: e.tensor_add(
            sst[:, 2:3], sst[:, 2:3], sst[:, 0:1]), waits=[mx, nm])
        am2 = S.comp("dve", lambda e: e.scalar_tensor_tensor(
            out=sst[:, 3:4], in0=sst[:, 3:4], scalar=-1.0, in1=sst[:, 0:1],
            op0=ALU.mult, op1=ALU.subtract), waits=[mn, nm])
        am = S.comp("dve", lambda e: e.tensor_max(
            sst[:, 2:3], sst[:, 2:3], sst[:, 3:4]), waits=[am1, am2])
        amr = S.comp("dve", lambda e: e.tensor_scalar(
            out=sst[:, 2:3], in0=sst[:, 2:3], scalar1=sst[:, 1:2], scalar2=EPS,
            op0=ALU.mult, op1=ALU.max), waits=[am, rs])
        rec = S.comp("dve", lambda e: e.reciprocal(sst[:, 4:5], sst[:, 2:3]),
                     waits=[amr])
        sc = S.comp("dve", lambda e: e.tensor_scalar_mul(
            sst[:, 4:5], sst[:, 4:5], 127.0), waits=[rec])
        a_ = S.comp("dve", lambda e, tb=tb: e.tensor_mul(
            q2_a[:, tb:tb + 1], sst[:, 1:2], sst[:, 4:5]), waits=[sc],
            tag=f"q2a_{tb}")
        b_ = S.comp("dve", lambda e, tb=tb: e.tensor_mul(
            q2_b[:, tb:tb + 1], sst[:, 0:1], q2_a[:, tb:tb + 1]),
            waits=[a_], tag=f"q2b_{tb}")
        iq = S.comp("dve", lambda e, tb=tb: e.tensor_mul(
            iq2[:, tb:tb + 1], sst[:, 4:5], ws2v[:, 0:1]), waits=[sc, tw2],
            tag=f"iq2_{tb}")
        S.comp("dve", lambda e, tb=tb: e.reciprocal(
            dq2[:, tb:tb + 1], iq2[:, tb:tb + 1]), waits=[iq], tag=f"dq2_{tb}")
        dg2 = S.comp("dve", lambda e, tb=tb: e.tensor_scalar(
            out=diagb, in0=ident, scalar1=iq2[:, tb:tb + 1], scalar2=None,
            op0=ALU.mult),
            waits=[f"iq2_{tb}"] + opt(f"idr2t_{tb-1}", f"idr1t_{NTB-1}"),
            tag=f"idr2d_{tb}")
        tp0 = S.comp("pe", lambda e, tb=tb: e.matmul(
            pb[5][0:1, (tb % 4) * 128:(tb % 4) * 128 + 128],
            ones[:, 0:1], diagb, start=True, stop=True),
            waits=[dg2] + opt(f"idr2_{tb-4}", f"idr1_{NTB-1}"),
            tag=f"idr2t_{tb}")
        S.comp("dve", lambda e, tb=tb: e.tensor_copy(
            idr2[0:1, tb * 128:(tb + 1) * 128],
            pb[5][0:1, (tb % 4) * 128:(tb % 4) * 128 + 128]), waits=[tp0],
            tag=f"idr2_{tb}")
        evs = []
        for q in range(4):
            lt = S.dma("sp", "grl", lambda e, tb=tb, q=q: e.dma_start(
                out=gr, in_=g_h[tb, :, q * QW:(q + 1) * QW]),
                waits=opt(f"xq2q_{tb}_{q-1}", f"xq2q_{tb-1}_3", "bar_f2"),
                tag=f"grl_{tb}_{q}")
            w2_ = S.comp("dve", lambda e, tb=tb: e.tensor_scalar(
                out=wb3, in0=gr, scalar1=q2_a[:, tb:tb + 1],
                scalar2=q2_b[:, tb:tb + 1], op0=ALU.mult, op1=ALU.add),
                waits=[b_, lt] + opt(f"xq2q_{tb}_{q-1}", f"xq2q_{tb-1}_3"),
                tag=f"xq2w_{tb}_{q}")
            kcv = S.comp("dve", lambda e: e.tensor_copy(ki3, wb3),
                         waits=[w2_], tag=f"xq2k_{tb}_{q}")
            dd = S.comp("dve", lambda e: e.tensor_sub(mb3, wb3, ki3),
                        waits=[kcv], tag=f"xq2d_{tb}_{q}")
            cc = S.comp("dve", lambda e: e.tensor_scalar(
                out=mb3, in0=mb3, scalar1=magp[:, 0:1], scalar2=magn[:, 0:1],
                op0=ALU.add, op1=ALU.add), waits=[dd, "magp", "magn"],
                tag=f"xq2c_{tb}_{q}")
            xq = S.comp("dve", lambda e: e.tensor_add(xq2b, mb3, ki3),
                        waits=[cc] + opt(f"xq2tp_{tb}_{q*16-1}",
                                         f"xq2tp_{tb-1}_{NFC-1}"),
                        tag=f"xq2q_{tb}_{q}")
            for fl in range(16):
                fc = q * 16 + fl
                bank = pbf[fc % 2]
                tp = S.comp("pe", lambda e, fl=fl, fc=fc, bank=bank: e.matmul(
                    bank[:, 0:128], xq2b[:, fl * 128:(fl + 1) * 128],
                    identb[:, :], start=True, stop=True, is_transpose=True),
                    waits=[xq] + (opt(f"xq2ev_{tb}_{fc-2}") if fc >= 2 else
                                  opt(f"xq2ev_{tb-1}_{NFC-2+fc}", "bar_f2")),
                    tag=f"xq2tp_{tb}_{fc}")
                ev = S.comp("act", lambda e, fc=fc, bank=bank: e.activation(
                    out=stg3[:, fc * 128:(fc + 1) * 128], in_=bank[:, 0:128],
                    func=AF.Copy),
                    waits=[tp] + opt(f"xq2st_{tb-1}" if fc == 0 else "~"),
                    tag=f"xq2ev_{tb}_{fc}")
                if fl == 15:
                    evs.append(ev)
        S.dma("pool", "xq2s", lambda e, tb=tb: e.dma_start(
            out=xq2t_h[tb, :, :], in_=stg3), waits=evs, tag=f"xq2st_{tb}")

    F4 = F3
    b2r = sb(F4, "b2rT", 1, D)
    wq2s = sb(F4, "wq2s", 128, NFC * 512, BF16)
    xq2r = sb(F4, "xq2r", 128, 2 * NFC * 128, BF16)
    y2b = sb(F4, "y2b", 128, 512)

    S.dma("sp", "cb2", lambda e: e.dma_start(out=b2r, in_=b2_h[:, :]),
          tag="ld_b2", waits=["bar_f2"])

    for dq in range(NDQ):
        lws = []
        for j in range(4):
            si = dq * 4 + j
            lws.append(S.dma("sp", f"wq2l{j}", lambda e, si=si, j=j: e.dma_start(
                out=wq2s.rearrange("p (a b) -> p a b", b=512)[:, :, j * 128:(j + 1) * 128],
                in_=wq2t_h[si, :, :].rearrange("p (a b) -> p a b", b=128)),
                waits=opt(f"mm2_{dq-1}_{NTB-1}", "bar_f2"),
                tag=f"wq2l_{dq}_{j}"))
        for tb in range(NTB):
            sl = tb % 2
            lx = S.dma("sp", f"xq2l{sl}", lambda e, tb=tb, sl=sl: e.dma_start(
                out=xq2r[:, sl * NFC * 128:(sl + 1) * NFC * 128],
                in_=xq2t_h[tb, :, :]),
                waits=[f"xq2st_{tb}"] +
                      opt(f"mm2_{dq}_{tb-2}", f"mm2_{dq-1}_{NTB+tb-2}"),
                tag=f"xq2l_{dq}_{tb}")
            bank = pb[2 + tb % 2]
            mm = S.comp("pe", lambda e, dq=dq, tb=tb, sl=sl, bank=bank: [
                e.matmul(bank[:, 0:512],
                         xq2r[:, sl * NFC * 128 + fc * 128:
                              sl * NFC * 128 + (fc + 1) * 128],
                         wq2s[:, fc * 512:(fc + 1) * 512],
                         start=(fc == 0), stop=False)
                for fc in range(NFC)] + [
                e.matmul(bank[:, 0:512], idr2[0:1, tb * 128:(tb + 1) * 128],
                         b2r[0:1, dq * 512:(dq + 1) * 512],
                         start=False, stop=True)],
                waits=lws + [lx, f"idr2_{tb}", "ld_b2"] +
                      opt(f"y2ev_{dq}_{tb-2}", f"y2ev_{dq-1}_{NTB+tb-2}"),
                n=NFC + 1, tag=f"mm2_{dq}_{tb}")
            ye = S.comp("act", lambda e, tb=tb, bank=bank: e.activation(
                out=y2b, in_=bank[:, 0:512], func=AF.Copy,
                scale=dq2[:, tb:tb + 1]),
                waits=[mm, f"dq2_{tb}"] + opt(f"y2st_{dq}_{tb-1}",
                                              f"y2st_{dq-1}_{NTB-1}"),
                tag=f"y2ev_{dq}_{tb}")
            S.dma("pool", "y2s", lambda e, dq=dq, tb=tb: e.dma_start(
                out=y2_h[tb, :, dq * 512:(dq + 1) * 512], in_=y2b),
                waits=[ye], tag=f"y2st_{dq}_{tb}")


    F5 = F4
    y2r = sb(F5, "y2r", 128, D)
    outb = sb(F5, "outb", 128, D)

    for tb in range(NTB):
        lt = S.dma("sp", "y2l", lambda e, tb=tb: e.dma_start(
            out=y2r, in_=y2_h[tb, :, :]), waits=[f"y2st_{NDQ-1}_{tb}"] + opt(f"oaf_{tb-1}"),
            tag=f"y2l_{tb}")
        lr = S.dma("sp", "r1l2", lambda e, tb=tb: e.dma_start(
            out=outb, in_=r1_h[tb, :, :]),
            waits=opt(f"ost_{tb-1}"), tag=f"r1l2_{tb}")
        bs = S.comp("dve", lambda e: [
            e.bn_stats(bnst[:, 6 * i:6 * (i + 1)], y2r[:, 512 * i:512 * (i + 1)])
            for i in range(4)], waits=[lt] + opt(f"oaf_{tb-1}"), n=4)
        ag = S.comp("dve", lambda e: e.bn_aggr(bnag, bnst[:, 0:24]), waits=[bs] + agw())
        nm = S.comp("dve", lambda e: e.tensor_scalar_mul(
            sst[:, 0:1], bnag[:, 0:1], -1.0), waits=[ag])
        rs = rstd(ag, sst[:, 1:2])
        bb_ = S.comp("dve", lambda e: e.tensor_mul(
            sst[:, 2:3], sst[:, 0:1], sst[:, 1:2]), waits=[nm, rs])
        af0 = S.comp("dve", lambda e: e.tensor_scalar(
            out=y2r, in0=y2r, scalar1=sst[:, 1:2], scalar2=sst[:, 2:3],
            op0=ALU.mult, op1=ALU.add), waits=[bb_], tag=f"oaa_{tb}")
        af = S.comp("dve", lambda e: e.tensor_add(outb, y2r, outb),
                    waits=[af0, lr], tag=f"oaf_{tb}")
        S.dma("pool", "outs", lambda e, tb=tb: e.dma_start(
            out=out_h[tb * 128:(tb + 1) * 128, :], in_=outb),
            waits=[af], tag=f"ost_{tb}")

    S.comp("pool", lambda e: e.engine_nop(), waits=[f"ost_{NTB-1}"])

    S.emit(nc, stack)
    F5.close()
    ID2.close()
    GS.close()
    stack.close()
    return nc


def _host_prep(L0_attn_A_log, L0_attn_D):
    import ml_dtypes
    alog = np.asarray(L0_attn_A_log, dtype=np.float32)
    dvec = np.asarray(L0_attn_D, dtype=np.float32)
    app = np.ascontiguousarray(
        alog.reshape(ND, 128, NST).transpose(1, 0, 2).reshape(128, ND * NST))
    dpp = np.ascontiguousarray(dvec.reshape(ND, 128).T)
    ident = np.eye(128, dtype=np.float32)
    identb = np.eye(128, dtype=np.float32).astype(ml_dtypes.bfloat16)
    ones = np.ones((128, 512), dtype=np.float32)
    repl = np.zeros((32, 32 * 128), dtype=np.float32)
    for k in range(32):
        repl[k, k * 128:(k + 1) * 128] = 1.0
    return app, dpp, ident, identb, ones, repl


def kernel(L0_attn_x, L0_attn_A_log, L0_attn_D, L0_attn_W_xproj,
           L0_ffn_w1, L0_ffn_b1, L0_ffn_w2, L0_ffn_b2):
    x = np.ascontiguousarray(np.asarray(L0_attn_x, dtype=np.float32))
    app, dpp, ident, identb, ones, repl = _host_prep(L0_attn_A_log, L0_attn_D)
    shared = {
        "wxp": np.ascontiguousarray(np.asarray(L0_attn_W_xproj, np.float32)),
        "app": app, "dpp": dpp,
        "w1": np.ascontiguousarray(np.asarray(L0_ffn_w1, np.float32)),
        "b1r": np.ascontiguousarray(np.asarray(L0_ffn_b1, np.float32)[None, :]),
        "w2": np.ascontiguousarray(np.asarray(L0_ffn_w2, np.float32)),
        "b2r": np.ascontiguousarray(np.asarray(L0_ffn_b2, np.float32)[None, :]),
        "ident": ident, "identb": identb, "ones": ones, "repl": repl,
    }
    nc = build_nc()
    in_maps = [dict(shared, x=np.ascontiguousarray(x[b])) for b in range(8)]
    res = run_bass_kernel_spmd(nc, in_maps, core_ids=list(range(8)))
    out = np.stack([res.results[b]["out"] for b in range(8)], axis=0)
    return out.astype(np.float32)


# version 19 (final)
# speedup vs baseline: 1.1584x; 1.1584x over previous
import contextlib

import numpy as np

import concourse.bass as bass
import concourse.mybir as mybir
from concourse.bass_utils import run_bass_kernel_spmd

F32 = mybir.dt.float32
BF16 = mybir.dt.bfloat16
AF = mybir.ActivationFunctionType
ALU = mybir.AluOpType
AX = mybir.AxisListType

T = 2048
D = 2048
NST = 16
E = 2080
FF = 8192
TC = 256
NCH = T // TC
NTT = TC // 128
NTB = T // 128
ND = D // 128
NEB = 17
NFS = FF // 512
NFC = FF // 128
NDQ = 4
EPS = 1e-5
EPS_W = 1e-8

ENGS = ("sp", "act", "dve", "pool", "pe")
STOP_PHASE = None


class Sched:

    def __init__(self):
        self.steps = []
        self.cum = {}
        self.tags = {}
        self.last = {}
        self.pending_fence = None
        self.fence_seen = set()
        self.ntag = 0

    def _push(self, eng, stream, fn, waits, tag, inc):
        waits = list(waits)
        if self.pending_fence is not None and eng not in self.fence_seen:
            waits.append(self.pending_fence)
            self.fence_seen.add(eng)
        for wtag in waits:
            assert wtag in self.tags, f"unknown wait tag {wtag}"
        v = self.cum.get(stream, 0) + inc
        self.cum[stream] = v
        if tag is None:
            tag = f"_s{self.ntag}"
        self.ntag += 1
        assert tag not in self.tags, tag
        self.tags[tag] = (stream, v)
        self.last[stream] = tag
        self.steps.append((eng, stream, fn, tuple(waits), inc))
        return tag

    def comp(self, eng, fn, tag=None, waits=(), n=1):
        return self._push(eng, eng, fn, waits, tag, n)

    def dma(self, eng, stream, fn, tag=None, waits=(), n=1):
        return self._push(eng, "D" + stream, fn, waits, tag, 16 * n)

    def opt(self, *tags):
        return [t for t in tags if t in self.tags]

    def barrier(self, name):
        waits = list(self.last.values())
        self.comp("pool", lambda e: e.engine_nop(), tag=name, waits=waits)
        self.pending_fence = name
        self.fence_seen = {"pool"}

    def emit(self, nc, stack):
        streams = sorted({s for (_, s, _, _, _) in self.steps})
        sems = {s: stack.enter_context(nc.semaphore(f"sm_{i}"))
                for i, s in enumerate(streams)}
        per_eng = {e: [] for e in ENGS}
        for st in self.steps:
            per_eng[st[0]].append(st)
        watermark = {}

        def runner(eng_name):
            def run(eng_obj):
                for (eng, stream, fn, waits, inc) in per_eng[eng_name]:
                    for wtag in waits:
                        ws, wv = self.tags[wtag]
                        key = (eng_name, ws)
                        if watermark.get(key, -1) < wv:
                            eng_obj.wait_ge(sems[ws], wv)
                            watermark[key] = wv
                    insts = fn(eng_obj)
                    if insts is None:
                        insts = []
                    if not isinstance(insts, (list, tuple)):
                        insts = [insts]
                    per = 16 if stream.startswith("D") else 1
                    assert len(insts) * per == inc, (stream, len(insts), inc)
                    for i in insts:
                        i.then_inc(sems[stream], per)
            return run

        with nc.Block() as block:
            block.sync(runner("sp"))
            block.scalar(runner("act"))
            block.vector(runner("dve"))
            block.gpsimd(runner("pool"))
            block.tensor(runner("pe"))


def build_nc():
    nc = bass.Bass("TRN2", target_bir_lowering=False)
    S = Sched()
    opt = S.opt
    rsx = {"last": None}

    x_h = nc.dram_tensor("x", [T, D], F32, kind="ExternalInput").ap()
    wxp_h = nc.dram_tensor("wxp", [E, D], F32, kind="ExternalInput").ap()
    app_h = nc.dram_tensor("app", [128, ND * NST], F32, kind="ExternalInput").ap()
    dpp_h = nc.dram_tensor("dpp", [128, ND], F32, kind="ExternalInput").ap()
    w1_h = nc.dram_tensor("w1", [FF, D], F32, kind="ExternalInput").ap()
    b1_h = nc.dram_tensor("b1r", [1, FF], F32, kind="ExternalInput").ap()
    w2_h = nc.dram_tensor("w2", [D, FF], F32, kind="ExternalInput").ap()
    b2_h = nc.dram_tensor("b2r", [1, D], F32, kind="ExternalInput").ap()
    ident_h = nc.dram_tensor("ident", [128, 128], F32, kind="ExternalInput").ap()
    identb_h = nc.dram_tensor("identb", [128, 128], BF16, kind="ExternalInput").ap()
    ones_h = nc.dram_tensor("ones", [128, 512], F32, kind="ExternalInput").ap()
    repl_h = nc.dram_tensor("repl", [32, 32 * 128], F32, kind="ExternalInput").ap()
    out_h = nc.dram_tensor("out", [T, D], F32, kind="ExternalOutput").ap()

    wxpt_h = nc.dram_tensor("wxpt_s", [NEB, 128, ND * 128], F32, kind="Internal").ap()
    wq1t_h = nc.dram_tensor("wq1t_s", [NFS, 128, ND * 512], BF16, kind="Internal").ap()
    wq2t_h = nc.dram_tensor("wq2t_s", [16, 128, NFC * 128], BF16, kind="Internal").ap()
    r1_h = nc.dram_tensor("r1_s", [NTB, 128, D], F32, kind="Internal").ap()
    g_h = nc.dram_tensor("g_s", [NTB, 128, FF], F32, kind="Internal").ap()
    xq2t_h = nc.dram_tensor("xq2t_s", [NTB, 128, NFC * 128], BF16, kind="Internal").ap()
    y2_h = nc.dram_tensor("y2_s", [NTB, 128, D], F32, kind="Internal").ap()

    stack = contextlib.ExitStack()

    def sb(st, name, p, c, dt=F32):
        return st.enter_context(nc.sbuf_tensor(name, [p, c], dt))[:, :]

    ident = sb(stack, "identT", 128, 128)
    identb = sb(stack, "identbT", 128, 128, BF16)
    ones = sb(stack, "onesT", 128, 512)
    app = sb(stack, "appT", 128, ND * NST)
    aneg = sb(stack, "anegT", 128, ND * NST)
    dpp = sb(stack, "dppT", 128, ND)
    q1_a = sb(stack, "q1_a", 128, NTB)
    q1_b = sb(stack, "q1_b", 128, NTB)
    dq1 = sb(stack, "dq1", 128, NTB)
    iq1 = sb(stack, "iq1", 128, NTB)
    q2_a = sb(stack, "q2_a", 128, NTB)
    q2_b = sb(stack, "q2_b", 128, NTB)
    dq2 = sb(stack, "dq2", 128, NTB)
    iq2 = sb(stack, "iq2", 128, NTB)
    bnst = sb(stack, "bnst", 128, 24)
    bnag = sb(stack, "bnag", 128, 2)
    sst = sb(stack, "sst", 128, 8)
    rtmp = sb(stack, "rtmp", 128, 2)
    diagb = sb(stack, "diagb", 128, 128)
    epsc = sb(stack, "epsc", 128, 1)
    halfn = sb(stack, "halfn", 128, 1)
    magp = sb(stack, "magp", 128, 1)
    magn = sb(stack, "magn", 128, 1)
    wsum = sb(stack, "wsum", 128, 96)
    ws1v = sb(stack, "ws1v", 128, 2)
    ws2v = sb(stack, "ws2v", 128, 2)
    wsS = sb(stack, "wsS", 1, 8)
    zmx = sb(stack, "zmx", 128, NTB)
    zmn = sb(stack, "zmn", 128, NTB)

    pb = [stack.enter_context(nc.psum_tensor(f"pb{i}", [128, 512], F32))[:, :]
          for i in range(6)]
    pbf = [stack.enter_context(nc.psum_tensor(f"pbf{i}", [128, 1024], BF16))[:, :]
           for i in range(2)]

    S.comp("dve", lambda e: e.memset(epsc, EPS), tag="epsc")
    S.comp("dve", lambda e: e.memset(magp, 6755399441055744.0), tag="magp")
    S.comp("dve", lambda e: e.memset(magn, -6755399441055744.0), tag="magn")
    S.comp("dve", lambda e: e.memset(halfn, -0.5), tag="halfn")

    def probe_finish(buf):
        S.barrier("probe_bar")
        S.dma("pool", "outs", lambda e: e.dma_start(
            out=out_h[0:128, 0:128], in_=buf), tag="probe_st",
            waits=["probe_bar"])
        S.comp("pool", lambda e: e.engine_nop(), waits=["probe_st"])
        S.emit(nc, stack)
        return nc

    def rstd(ag_tag, out_ap, tag=None):
        w = [ag_tag, "epsc"] + ([rsx["last"]] if rsx["last"] else [])
        t1 = S.comp("act", lambda e: e.activation(
            out=rtmp[:, 0:1], in_=bnag[:, 1:2], func=AF.Ln,
            bias=epsc[:, 0:1]), waits=w)
        t2 = S.comp("act", lambda e, o=out_ap: e.activation(
            out=o, in_=rtmp[:, 0:1], func=AF.Exp, scale=halfn[:, 0:1]),
            waits=[t1, "halfn"], tag=tag)
        rsx["last"] = t2
        return t2

    def agw():
        return [rsx["last"]] if rsx["last"] else []

    for dst, src, nm in ((ident, ident_h, "ident"), (identb, identb_h, "identb"),
                         (ones, ones_h, "ones"), (app, app_h, "app"),
                         (dpp, dpp_h, "dpp")):
        S.dma("sp", "c" + nm, lambda e, d=dst, s=src: e.dma_start(
            out=d, in_=s[:, :]), tag="ld_" + nm)

    t_e = S.comp("act", lambda e: e.activation(out=aneg, in_=app, func=AF.Exp),
                 waits=["ld_app"])
    S.comp("dve", lambda e: e.tensor_scalar_mul(aneg, aneg, -1.0),
           waits=[t_e], tag="aneg")

    W0 = contextlib.ExitStack()
    wxs = sb(W0, "wxs", 128, 2 * D)
    wxo = sb(W0, "wxo", 128, ND * 128)

    for eb in range(NEB):
        sl = eb % 2
        rows = 128 if eb < 16 else 32
        lt = S.dma("sp", f"wxs{sl}", lambda e, eb=eb, sl=sl, rows=rows:
                   e.dma_start(out=wxs[0:rows, sl * D:sl * D + D],
                               in_=wxp_h[eb * 128:eb * 128 + rows, :]),
                   waits=opt(f"wxt{eb-2}q3"), tag=f"wxl{eb}")
        evs = []
        for q in range(4):
            bank = pb[4 + (q % 2)]
            tt = S.comp("pe", lambda e, sl=sl, rows=rows, q=q, bank=bank: [
                e.matmul(bank[0:128, k * 128:k * 128 + rows],
                         wxs[0:rows, sl * D + (q * 4 + k) * 128:
                             sl * D + (q * 4 + k + 1) * 128],
                         ident[0:rows, 0:rows],
                         start=True, stop=True, is_transpose=True)
                for k in range(4)],
                waits=[lt, "ld_ident"] + (opt(f"wxe{eb}q{q-2}") if q >= 2 else
                              opt(f"wxe{eb-1}q{q+2}")),
                n=4, tag=f"wxt{eb}q{q}")
            ev = S.comp("dve", lambda e, q=q, bank=bank: e.tensor_copy(
                wxo[:, q * 512:(q + 1) * 512], bank[:, 0:512]),
                waits=[tt] + opt(f"wxst{eb-1}"), tag=f"wxe{eb}q{q}")
            evs.append(ev)
        S.dma("pool", "wxpt", lambda e, eb=eb: e.dma_start(
            out=wxpt_h[eb, :, :], in_=wxo), waits=evs, tag=f"wxst{eb}")

    W1 = contextlib.ExitStack()
    ws = sb(W1, "wsl", 128, 2 * 4096)
    wabs = sb(W1, "wabs", 128, 4096)

    def wmean(nsl, cols, col0, pfx, src_fn, prev2):
        for si in range(nsl):
            sl = si % 2
            lt = S.dma("sp", f"wsl{sl}", lambda e, si=si, sl=sl, cols=cols:
                       e.dma_start(out=ws[:, sl * 4096:sl * 4096 + cols],
                                   in_=src_fn(si)),
                       waits=opt(f"{pfx}a{si-2}") +
                             ([] if si >= 2 else opt(*prev2(si))),
                       tag=f"{pfx}l{si}")
            S.comp("act", lambda e, si=si, sl=sl, cols=cols, col0=col0:
                   e.activation(out=wabs[:, 0:cols],
                                in_=ws[:, sl * 4096:sl * 4096 + cols],
                                func=AF.Abs,
                                accum_out=wsum[:, col0 + si:col0 + si + 1]),
                   waits=[lt] + opt(f"{pfx}a{si-1}") +
                         ([] if si >= 1 else opt(*prev2(-1))),
                   tag=f"{pfx}a{si}")
        return f"{pfx}a{nsl-1}"

    tm1 = wmean(64, D, 0, "m1_",
                lambda si: w1_h[si * 128:(si + 1) * 128, :], lambda si: ())
    tm2 = wmean(32, 4096, 64, "m2_",
                lambda si: w2_h[(si // 2) * 128:(si // 2 + 1) * 128,
                                (si % 2) * 4096:(si % 2 + 1) * 4096],
                lambda si: (f"m1_a{62+si}" if si >= 0 else "m1_a63",))

    def wfin(col0, nsl, wsv, idx, wm):
        rt = S.comp("dve", lambda e, col0=col0, nsl=nsl, idx=idx: e.reduce_sum(
            sst[:, idx:idx + 1], wsum[:, col0:col0 + nsl], AX.X),
            waits=[wm], tag=f"wred{idx}")
        mt = S.comp("pe", lambda e, idx=idx: e.matmul(
            pb[5][0:1, idx:idx + 1], sst[:, idx:idx + 1], ones[:, 0:1],
            start=True, stop=True),
            waits=[rt, "ld_ones"] + opt("wbc0", f"wxt{NEB-1}q3"),
            tag=f"wps{idx}")
        et = S.comp("dve", lambda e, idx=idx: e.tensor_scalar(
            out=wsS[0:1, idx * 2:idx * 2 + 1], in0=pb[5][0:1, idx:idx + 1],
            scalar1=1.0 / (1 << 24), scalar2=EPS_W, op0=ALU.mult, op1=ALU.max),
            waits=[mt], tag=f"wsc{idx}")
        it = S.comp("dve", lambda e, idx=idx: e.reciprocal(
            wsS[0:1, idx * 2 + 1:idx * 2 + 2], wsS[0:1, idx * 2:idx * 2 + 1]),
            waits=[et], tag=f"winv{idx}")
        bt = S.comp("pe", lambda e, idx=idx: e.matmul(
            pb[5][0:128, 4 + idx:5 + idx], ones[0:1, 0:128],
            wsS[0:1, idx * 2 + 1:idx * 2 + 2], start=True, stop=True),
            waits=[it], tag=f"wbc{idx}")
        ct = S.comp("dve", lambda e, idx=idx, wsv=wsv: e.tensor_copy(
            wsv[:, 0:1], pb[5][:, 4 + idx:5 + idx]), waits=[bt])
        return S.comp("dve", lambda e, wsv=wsv: e.tensor_scalar_mul(
            wsv[:, 1:2], wsv[:, 0:1], -1.0), waits=[ct], tag=f"wneg{idx}")

    tw1 = wfin(0, 64, ws1v, 0, tm1)
    tw2 = wfin(64, 32, ws2v, 1, tm2)

    wqb = sb(W1, "wqb", 128, 4096, BF16)
    wc1 = sb(W1, "wc1", 128, 4096)
    stg1 = sb(W1, "stg1", 128, ND * 512, BF16)
    stg2 = sb(W1, "stg2", 128, NFC * 128, BF16)

    for fs in range(NFS):
        evs_all = []
        for j in range(4):
            si = fs * 4 + j
            sl = si % 2
            lt = S.dma("sp", f"wsl{sl}", lambda e, si=si, sl=sl: e.dma_start(
                out=ws[:, sl * 4096:sl * 4096 + D],
                in_=w1_h[si * 128:(si + 1) * 128, :]),
                waits=opt(f"w1q2_{si-2}") +
                      ([] if si >= 2 else opt(f"m2_a{30+si}")),
                tag=f"w1l{si}")
            c1 = S.comp("dve", lambda e, sl=sl: e.tensor_scalar(
                out=wc1[:, 0:D], in0=ws[:, sl * 4096:sl * 4096 + D],
                scalar1=ws1v[:, 0:1], scalar2=0.5, op0=ALU.mult, op1=ALU.is_ge),
                waits=[lt, tw1] + opt(f"w1qm_{si-1}"), tag=f"w1q1_{si}")
            c2 = S.comp("dve", lambda e, sl=sl: e.tensor_scalar(
                out=wabs[:, 0:D], in0=ws[:, sl * 4096:sl * 4096 + D],
                scalar1=ws1v[:, 1:2], scalar2=0.5, op0=ALU.mult, op1=ALU.is_gt),
                waits=[c1] + opt(*(["m2_a31"] if si == 0 else ())),
                tag=f"w1q2_{si}")
            qm = S.comp("dve", lambda e: e.tensor_sub(
                wqb[:, 0:D], wc1[:, 0:D], wabs[:, 0:D]),
                waits=[c2] + opt(f"w1tp_{si-1}_{ND-1}"), tag=f"w1qm_{si}")
            for dc in range(ND):
                bank = pbf[dc % 2]
                tp = S.comp("pe", lambda e, dc=dc, bank=bank: e.matmul(
                    bank[:, 0:128], wqb[:, dc * 128:(dc + 1) * 128],
                    identb[:, :], start=True, stop=True, is_transpose=True),
                    waits=[qm, "ld_identb"] +
                          (opt(f"w1ev_{si}_{dc-2}") if dc >= 2 else
                           opt(f"w1ev_{si-1}_{ND-2+dc}")),
                    tag=f"w1tp_{si}_{dc}")
                ev = S.comp("act", lambda e, dc=dc, j=j, bank=bank: e.activation(
                    out=stg1[:, dc * 512 + j * 128:dc * 512 + j * 128 + 128],
                    in_=bank[:, 0:128], func=AF.Copy),
                    waits=[tp] + opt(f"w1st_{fs-1}" if j == 0 and dc == 0 else "~"),
                    tag=f"w1ev_{si}_{dc}")
                if j == 3:
                    evs_all.append(ev)
        S.dma("pool", "wq1", lambda e, fs=fs: e.dma_start(
            out=wq1t_h[fs, :, :], in_=stg1),
            waits=evs_all + [f"w1ev_{fs*4+j}_{ND-1}" for j in range(3)],
            tag=f"w1st_{fs}")

    for si in range(16):
        evs_all = []
        for h in range(2):
            s2 = si * 2 + h
            sl = s2 % 2
            lt = S.dma("sp", f"wsl{sl}", lambda e, si=si, h=h, sl=sl: e.dma_start(
                out=ws[:, sl * 4096:(sl + 1) * 4096],
                in_=w2_h[si * 128:(si + 1) * 128, h * 4096:(h + 1) * 4096]),
                waits=opt(f"w2q2_{s2-2}") +
                      ([] if s2 >= 2 else opt(f"w1q2_{62+s2}")),
                tag=f"w2l{s2}")
            c1 = S.comp("dve", lambda e, sl=sl: e.tensor_scalar(
                out=wc1[:, :], in0=ws[:, sl * 4096:(sl + 1) * 4096],
                scalar1=ws2v[:, 0:1], scalar2=0.5, op0=ALU.mult, op1=ALU.is_ge),
                waits=[lt, tw2] + opt(f"w2qm_{s2-1}" if s2 >= 1 else "w1qm_63"),
                tag=f"w2q1_{s2}")
            c2 = S.comp("dve", lambda e, sl=sl: e.tensor_scalar(
                out=wabs[:, :], in0=ws[:, sl * 4096:(sl + 1) * 4096],
                scalar1=ws2v[:, 1:2], scalar2=0.5, op0=ALU.mult, op1=ALU.is_gt),
                waits=[c1], tag=f"w2q2_{s2}")
            qm = S.comp("dve", lambda e: e.tensor_sub(
                wqb[:, :], wc1[:, :], wabs[:, :]),
                waits=[c2] + opt(f"w2tp_{s2-1}_31" if s2 >= 1 else
                                 f"w1tp_63_{ND-1}"), tag=f"w2qm_{s2}")
            for fl in range(32):
                fc = h * 32 + fl
                bank = pbf[fl % 2]
                tp = S.comp("pe", lambda e, fl=fl, bank=bank: e.matmul(
                    bank[:, 0:128], wqb[:, fl * 128:(fl + 1) * 128],
                    identb[:, :], start=True, stop=True, is_transpose=True),
                    waits=[qm] + (opt(f"w2ev_{s2}_{fl-2}") if fl >= 2 else
                                  opt(f"w2ev_{s2-1}_{30+fl}",
                                      f"w1ev_63_{ND-2+fl}")),
                    tag=f"w2tp_{s2}_{fl}")
                ev = S.comp("act", lambda e, fc=fc, bank=bank: e.activation(
                    out=stg2[:, fc * 128:(fc + 1) * 128], in_=bank[:, 0:128],
                    func=AF.Copy),
                    waits=[tp] + opt(f"w2st_{si-1}" if fl == 0 and h == 0 else "~"),
                    tag=f"w2ev_{s2}_{fl}")
                if fl % 8 == 7:
                    evs_all.append(ev)
        S.dma("pool", "wq2", lambda e, si=si: e.dma_start(
            out=wq2t_h[si, :, :], in_=stg2), waits=evs_all, tag=f"w2st_{si}")

    S.barrier("bar_w")
    if STOP_PHASE == "bar_w":
        return probe_finish(ident[0:128, 0:128])
    W1.close()
    W0.close()

    M = contextlib.ExitStack()
    repl = sb(M, "replT", 32, 32 * 128)
    xc = sb(M, "xc", 128, NTT * D)
    xlt = sb(M, "xlt", 128, ND * TC)
    dtb = sb(M, "dtb", 128, ND * TC)
    dxt = sb(M, "dxt", 128, ND * TC)
    bbf = sb(M, "bbf", 128, NST * TC)
    cbf = sb(M, "cbf", 128, NST * TC)
    hbf = sb(M, "hbf", 128, NST * TC)
    aub = sb(M, "aub", 128, 8 * TC)
    spt = sb(M, "spt", 128, 2 * TC)
    ybf = sb(M, "ybf", 128, NTT * D)
    r1b = sb(M, "r1b", 128, D)
    bcb = sb(M, "bcb", 32, TC)
    hst = sb(M, "hst", 128, ND * NST)
    dxx = sb(M, "dxx", 128, 2 * TC)
    hcb = sb(M, "hcb", 128, NST * TC)
    wxt = sb(M, "wxt", 128, 2 * ND * 128)

    S.dma("sp", "crepl", lambda e: e.dma_start(out=repl, in_=repl_h[:, :]),
          tag="ld_repl")
    S.comp("dve", lambda e: e.memset(hst, 0.0), tag="h0", waits=["bar_w"])

    for c in range(NCH):
        for tt in range(NTT):
            tb = c * NTT + tt
            xl = S.dma("sp", f"xch{tt}", lambda e, tb=tb, tt=tt: e.dma_start(
                out=xc[:, tt * D:(tt + 1) * D],
                in_=x_h[tb * 128:(tb + 1) * 128, :]),
                waits=opt(f"xtp{c-1}_{ND-1}"), tag=f"xl{c}_{tt}")
            bs = S.comp("dve", lambda e, tt=tt: [
                e.bn_stats(bnst[:, 6 * i:6 * (i + 1)],
                           xc[:, tt * D + 512 * i:tt * D + 512 * (i + 1)])
                for i in range(4)],
                waits=[xl] + opt(f"ln1ap{c-1}_{tt}", "bar_w"), n=4)
            ag = S.comp("dve", lambda e: e.bn_aggr(bnag, bnst[:, 0:24]),
                        waits=[bs] + agw())
            nm = S.comp("dve", lambda e: e.tensor_scalar_mul(
                sst[:, 0:1], bnag[:, 0:1], -1.0), waits=[ag])
            rs = rstd(ag, sst[:, 1:2])
            S.comp("dve", lambda e, tt=tt: e.tensor_scalar(
                out=xc[:, tt * D:(tt + 1) * D], in0=xc[:, tt * D:(tt + 1) * D],
                scalar1=sst[:, 0:1], scalar2=sst[:, 1:2],
                op0=ALU.add, op1=ALU.mult), waits=[nm, rs], tag=f"ln1ap{c}_{tt}")

        for dc in range(ND):
            bank = pb[4 + (dc % 2)]
            tt_ = S.comp("pe", lambda e, dc=dc, bank=bank: [
                e.matmul(bank[:, tt * 128:(tt + 1) * 128],
                         xc[:, tt * D + dc * 128:tt * D + (dc + 1) * 128],
                         ident[:, :], start=True, stop=True, is_transpose=True)
                for tt in range(NTT)],
                waits=[f"ln1ap{c}_{tt}" for tt in range(NTT)] + ["ld_ident"] +
                      (opt(f"xevq{c}_{dc-2}") if dc >= 2 else
                       opt(f"yev{c-1}_{ND-2+dc}_{NTT-1}")),
                n=NTT, tag=f"xtp{c}_{dc}")
            S.comp("dve", lambda e, dc=dc, bank=bank: e.tensor_copy(
                xlt[:, dc * TC:(dc + 1) * TC], bank[:, 0:TC]),
                waits=[tt_] + opt(f"xpj{c-1}_15", f"dxt{c-1}_{dc}"),
                tag=f"xevq{c}_{dc}")

        def emit_xpj(eb, p):
            erows = 128 if eb < 16 else 32
            sl = p % 2
            if p >= 2:
                pw = opt(f"xpj{c}_{16 if p == 2 else p - 3}")
            else:
                pw = opt(f"xpj{c-1}_{15 - p}", "bar_w")
            lw = S.dma("sp", f"wxt{sl}", lambda e, eb=eb, sl=sl: e.dma_start(
                out=wxt[:, sl * ND * 128:(sl + 1) * ND * 128],
                in_=wxpt_h[eb, :, :]),
                waits=pw, tag=f"wxtl{c}_{eb}")
            bank = pb[p % 2]
            if p >= 3:
                bw = opt(f"spe{c}_{p-3}")
            elif p == 2:
                bw = opt(f"bcc{c}_16")
            else:
                bw = opt(f"spe{c-1}_{15-p}")
            mm = S.comp("pe", lambda e, eb=eb, sl=sl, erows=erows, bank=bank: [
                e.matmul(bank[0:erows, 0:TC],
                         wxt[:, sl * ND * 128 + dc * 128:
                             sl * ND * 128 + dc * 128 + erows],
                         xlt[:, dc * TC:(dc + 1) * TC],
                         start=(dc == 0), stop=(dc == ND - 1))
                for dc in range(ND)],
                waits=[lw] + [f"xevq{c}_{dc}" for dc in range(ND)] + bw,
                n=ND, tag=f"xpj{c}_{eb}")
            if eb < 16:
                e1 = S.comp("act", lambda e, eb=eb, p=p, bank=bank: e.activation(
                    out=spt[:, (p % 2) * TC:(p % 2 + 1) * TC],
                    in_=bank[0:128, 0:TC], func=AF.Exp),
                    waits=[mm] + opt(f"spl{c}_{eb-2}") +
                          ([] if eb >= 2 else opt(f"spl{c-1}_{14+eb}")),
                    tag=f"spe{c}_{eb}")
                S.comp("act", lambda e, eb=eb, p=p: e.activation(
                    out=dtb[:, eb * TC:(eb + 1) * TC],
                    in_=spt[:, (p % 2) * TC:(p % 2 + 1) * TC],
                    func=AF.Ln, bias=1.0),
                    waits=[e1] + opt(f"dxt{c-1}_{eb}",
                                     f"aex{c-1}_{eb*NST+NST-1}"),
                    tag=f"spl{c}_{eb}")
            else:
                S.comp("act", lambda e, bank=bank: e.activation(
                    out=bcb[0:32, 0:TC], in_=bank[0:32, 0:TC], func=AF.Copy),
                    waits=[mm] + opt(f"bcm{c-1}_31"), tag=f"bcc{c}_{eb}")

        emit_xpj(16, 0)

        for k in range(32):
            half, n = divmod(k, 16)
            dstb = bbf if half == 0 else cbf
            bank = pb[2 + (k % 2)]
            bm = S.comp("pe", lambda e, k=k, bank=bank: e.matmul(
                bank[:, 0:TC], repl[0:32, k * 128:(k + 1) * 128],
                bcb[0:32, 0:TC], start=True, stop=True),
                waits=[f"bcc{c}_16", "ld_repl"] +
                      (opt(f"bcv{c}_{k-2}") if k >= 2 else
                       opt(f"bcv{c-1}_{30+k}")),
                tag=f"bcm{c}_{k}")
            wv = [bm]
            if k == 0:
                wv += opt(f"uml{c-1}_255")
            if k == 16:
                wv += opt(f"hcm{c-1}_{ND-1}_{NST-1}")
            S.comp("act", lambda e, n=n, bank=bank, dstb=dstb: e.activation(
                out=dstb[:, n * TC:(n + 1) * TC], in_=bank[:, 0:TC],
                func=AF.Copy), waits=wv, tag=f"bcv{c}_{k}")

        def emit_yev(dhy):
            bank = pb[4 + (dhy % 2)]
            for tt in range(NTT):
                S.comp("act", lambda e, dhy=dhy, tt=tt, bank=bank: e.activation(
                    out=ybf[:, tt * D + dhy * 128:tt * D + (dhy + 1) * 128],
                    in_=bank[:, tt * 128:(tt + 1) * 128], func=AF.Copy),
                    waits=[f"ytp{c}_{dhy}"] +
                          opt(f"r1af{c-1}_{tt}" if dhy == 0 else "~"),
                    tag=f"yev{c}_{dhy}_{tt}")

        emit_xpj(0, 1)
        emit_xpj(1, 2)
        for dh in range(ND):
            hb0 = 0
            S.comp("dve", lambda e, dh=dh: e.tensor_mul(
                dxt[:, dh * TC:(dh + 1) * TC],
                dtb[:, dh * TC:(dh + 1) * TC],
                xlt[:, dh * TC:(dh + 1) * TC]),
                waits=[f"spl{c}_{dh}", f"xevq{c}_{dh}"] +
                      opt(f"uml{c-1}_{dh*NST+NST-1}"), tag=f"dxt{c}_{dh}")
            for n in range(NST):
                g = dh * NST + n
                slot = g % 4
                at = S.comp("act", lambda e, dh=dh, g=g, slot=slot: e.activation(
                    out=aub[:, slot * TC:(slot + 1) * TC],
                    in_=dtb[:, dh * TC:(dh + 1) * TC], func=AF.Exp,
                    scale=aneg[:, g:g + 1]),
                    waits=[f"spl{c}_{dh}", "aneg"] +
                          opt(f"scan{c}_g{g-4}", f"scan{c-1}_g{NST*ND-4+g}"),
                    tag=f"aex{c}_{g}")
                eng_u = "pool" if g % 2 == 1 else "dve"
                ut = S.comp(eng_u, lambda e, dh=dh, n=n, slot=slot: e.tensor_mul(
                    aub[:, (4 + slot) * TC:(5 + slot) * TC],
                    dxt[:, dh * TC:(dh + 1) * TC],
                    bbf[:, n * TC:(n + 1) * TC]),
                    waits=[f"dxt{c}_{dh}", f"bcv{c}_{n}"] +
                          opt(f"scan{c}_g{g-4}", f"scan{c-1}_g{NST*ND-4+g}"),
                    tag=f"uml{c}_{g}")
                sw = [at, ut, "h0"] + opt(
                    f"hcm{c}_{dh-1}_{n}" if dh >= 1 else f"hcm{c-1}_{ND-1}_{n}")
                S.comp("dve", lambda e, g=g, n=n, slot=slot, hb0=hb0:
                       e.tensor_tensor_scan(
                    out=hbf[:, hb0 + n * TC:hb0 + (n + 1) * TC],
                    data0=aub[:, slot * TC:(slot + 1) * TC],
                    data1=aub[:, (4 + slot) * TC:(5 + slot) * TC],
                    initial=hst[:, g:g + 1], op0=ALU.mult, op1=ALU.add),
                    waits=sw, tag=f"scan{c}_g{g}")
            if dh >= 1:
                emit_yev(dh - 1)
            if dh + 2 < ND:
                emit_xpj(dh + 2, dh + 3)
            hc = S.comp("dve", lambda e, dh=dh, hb0=hb0: e.tensor_copy(
                hst[:, dh * NST:(dh + 1) * NST],
                hbf[:, hb0 + TC - 1:hb0 + NST * TC:TC]),
                waits=[f"scan{c}_g{dh*NST+NST-1}"], tag=f"hcp{c}_{dh}")
            for n in range(NST):
                eng_h = "pool" if n % 2 == 1 else "dve"
                S.comp(eng_h, lambda e, n=n, hb0=hb0: e.tensor_mul(
                    hcb[:, n * TC:(n + 1) * TC],
                    hbf[:, hb0 + n * TC:hb0 + (n + 1) * TC],
                    cbf[:, n * TC:(n + 1) * TC]),
                    waits=[hc, f"bcv{c}_{16+n}"] +
                          opt(f"ytp{c}_{dh-1}" if dh >= 1 else
                              f"ytp{c-1}_{ND-1}"),
                    tag=f"hcm{c}_{dh}_{n}")
            dxm = S.comp("dve", lambda e, dh=dh: e.tensor_scalar(
                out=dxx[:, (dh % 2) * TC:(dh % 2 + 1) * TC],
                in0=xlt[:, dh * TC:(dh + 1) * TC],
                scalar1=dpp[:, dh:dh + 1], scalar2=None, op0=ALU.mult),
                waits=[f"xevq{c}_{dh}", "ld_dpp"] +
                      opt(f"ytp{c}_{dh-2}" if dh >= 2 else f"ytp{c-1}_{ND-2+dh}"),
                tag=f"dxm{c}_{dh}")
            bank = pb[4 + (dh % 2)]
            yt = S.comp("pe", lambda e, dh=dh, bank=bank, hb0=hb0: [
                mmi
                for tt in range(NTT)
                for mmi in ([
                    e.matmul(bank[:, tt * 128:(tt + 1) * 128],
                             hcb[:, n * TC + tt * 128:
                                 n * TC + (tt + 1) * 128],
                             ident[:, :], start=(n == 0), stop=False,
                             is_transpose=True)
                    for n in range(NST)] + [
                    e.matmul(bank[:, tt * 128:(tt + 1) * 128],
                             dxx[:, (dh % 2) * TC + tt * 128:
                                 (dh % 2) * TC + (tt + 1) * 128],
                             ident[:, :], start=False, stop=True,
                             is_transpose=True)])],
                waits=[f"hcm{c}_{dh}_{n}" for n in range(NST)] + [dxm] +
                      (opt(f"yev{c}_{dh-2}_{NTT-1}") if dh >= 2 else
                       opt(f"xevq{c}_{14+dh}")),
                n=NST * NTT + NTT, tag=f"ytp{c}_{dh}")
        emit_yev(ND - 1)

        for tt in range(NTT):
            tb = c * NTT + tt
            bs = S.comp("dve", lambda e, tt=tt: [
                e.bn_stats(bnst[:, 6 * i:6 * (i + 1)],
                           ybf[:, tt * D + 512 * i:tt * D + 512 * (i + 1)])
                for i in range(4)],
                waits=[f"yev{c}_{dh}_{tt}" for dh in range(ND)], n=4)
            ag = S.comp("dve", lambda e: e.bn_aggr(bnag, bnst[:, 0:24]),
                        waits=[bs] + agw())
            nm = S.comp("dve", lambda e: e.tensor_scalar_mul(
                sst[:, 3:4], bnag[:, 0:1], -1.0), waits=[ag])
            rs = rstd(ag, sst[:, 4:5])
            bb_ = S.comp("dve", lambda e: e.tensor_mul(
                sst[:, 5:6], sst[:, 3:4], sst[:, 4:5]), waits=[nm, rs])
            rl = S.dma("sp", "xrl", lambda e, tb=tb: e.dma_start(
                out=r1b, in_=x_h[tb * 128:(tb + 1) * 128, :]),
                waits=opt(f"r1st_{c}_{tt-1}", f"r1st_{c-1}_{NTT-1}"),
                tag=f"xrl{c}_{tt}")
            af0 = S.comp("dve", lambda e, tt=tt: e.tensor_scalar(
                out=ybf[:, tt * D:(tt + 1) * D],
                in0=ybf[:, tt * D:(tt + 1) * D],
                scalar1=sst[:, 4:5], scalar2=sst[:, 5:6],
                op0=ALU.mult, op1=ALU.add),
                waits=[bb_], tag=f"r1aa{c}_{tt}")
            af = S.comp("dve", lambda e, tt=tt: e.tensor_add(
                r1b, ybf[:, tt * D:(tt + 1) * D], r1b),
                waits=[af0, rl], tag=f"r1af{c}_{tt}")
            S.dma("pool", "r1s", lambda e, tb=tb: e.dma_start(
                out=r1_h[tb, :, :], in_=r1b), waits=[af],
                tag=f"r1st_{c}_{tt}")

    S.barrier("bar_m")
    if STOP_PHASE == "bar_m":
        return probe_finish(ident[0:128, 0:128])
    M.close()

    F1 = contextlib.ExitStack()
    GS = contextlib.ExitStack()
    gmx = sb(GS, "gmx", 128, NTB * NFS)
    gmn = sb(GS, "gmn", 128, NTB * NFS)
    gbn = sb(GS, "gbn", 128, NTB * NFS * 6)
    xq1t_st = contextlib.ExitStack()
    xq1t = sb(xq1t_st, "xq1t", 128, ND * T, BF16)
    r1c = sb(F1, "r1c", 128, D)
    zb = sb(F1, "zb", 128, D)
    wb = sb(F1, "wb", 128, D)
    mb_ = sb(F1, "mb", 128, D)
    xqb = sb(F1, "xqb", 128, D, BF16)
    ki1 = sb(F1, "ki1", 128, D, mybir.dt.int32)
    idr1 = sb(F1, "idr1", 1, NTB * 128)

    for tb in range(NTB):
        lt = S.dma("sp", "r1l", lambda e, tb=tb: e.dma_start(
            out=r1c, in_=r1_h[tb, :, :]),
            waits=opt(f"z3_{tb-1}", "bar_m"), tag=f"r1l_{tb}")
        bs = S.comp("dve", lambda e: [
            e.bn_stats(bnst[:, 6 * i:6 * (i + 1)], r1c[:, 512 * i:512 * (i + 1)])
            for i in range(4)], waits=[lt] + opt(f"z3_{tb-1}", "bar_m"), n=4)
        ag = S.comp("dve", lambda e: e.bn_aggr(bnag, bnst[:, 0:24]), waits=[bs] + agw())
        nm = S.comp("dve", lambda e: e.tensor_scalar_mul(
            sst[:, 0:1], bnag[:, 0:1], -1.0), waits=[ag])
        rs = rstd(ag, sst[:, 1:2])
        zt = S.comp("dve", lambda e: e.tensor_scalar(
            out=zb, in0=r1c, scalar1=sst[:, 0:1], scalar2=sst[:, 1:2],
            op0=ALU.add, op1=ALU.mult), waits=[nm, rs], tag=f"z3_{tb}")
        bs4 = S.comp("dve", lambda e: [
            e.bn_stats(bnst[:, 6 * i:6 * (i + 1)], zb[:, 512 * i:512 * (i + 1)])
            for i in range(4)], waits=[zt], n=4)
        ag4 = S.comp("dve", lambda e: e.bn_aggr(bnag, bnst[:, 0:24]),
                     waits=[bs4] + agw())
        mx = S.comp("dve", lambda e, tb=tb: e.tensor_reduce(
            zmx[:, tb:tb + 1], zb, AX.X, ALU.max), waits=[zt])
        mn = S.comp("dve", lambda e, tb=tb: e.tensor_reduce(
            zmn[:, tb:tb + 1], zb, AX.X, ALU.min), waits=[zt])
        nm4 = S.comp("dve", lambda e: e.tensor_scalar_mul(
            sst[:, 0:1], bnag[:, 0:1], -1.0), waits=[ag4])
        rs4 = rstd(ag4, sst[:, 1:2])
        am1 = S.comp("dve", lambda e, tb=tb: e.tensor_add(
            sst[:, 2:3], zmx[:, tb:tb + 1], sst[:, 0:1]), waits=[mx, nm4])
        am2 = S.comp("dve", lambda e, tb=tb: e.scalar_tensor_tensor(
            out=sst[:, 3:4], in0=zmn[:, tb:tb + 1], scalar=-1.0,
            in1=sst[:, 0:1], op0=ALU.mult, op1=ALU.subtract), waits=[mn, nm4])
        am = S.comp("dve", lambda e: e.tensor_max(
            sst[:, 2:3], sst[:, 2:3], sst[:, 3:4]), waits=[am1, am2])
        amr = S.comp("dve", lambda e: e.tensor_scalar(
            out=sst[:, 2:3], in0=sst[:, 2:3], scalar1=sst[:, 1:2], scalar2=EPS,
            op0=ALU.mult, op1=ALU.max), waits=[am, rs4])
        rec = S.comp("dve", lambda e: e.reciprocal(sst[:, 4:5], sst[:, 2:3]),
                     waits=[amr])
        sc = S.comp("dve", lambda e: e.tensor_scalar_mul(
            sst[:, 4:5], sst[:, 4:5], 127.0), waits=[rec])
        a_ = S.comp("dve", lambda e, tb=tb: e.tensor_mul(
            q1_a[:, tb:tb + 1], sst[:, 1:2], sst[:, 4:5]), waits=[sc],
            tag=f"q1a_{tb}")
        b_ = S.comp("dve", lambda e, tb=tb: e.tensor_mul(
            q1_b[:, tb:tb + 1], sst[:, 0:1], q1_a[:, tb:tb + 1]),
            waits=[a_], tag=f"q1b_{tb}")
        iq = S.comp("dve", lambda e, tb=tb: e.tensor_mul(
            iq1[:, tb:tb + 1], sst[:, 4:5], ws1v[:, 0:1]), waits=[sc, tw1],
            tag=f"iq1_{tb}")
        S.comp("dve", lambda e, tb=tb: e.reciprocal(
            dq1[:, tb:tb + 1], iq1[:, tb:tb + 1]), waits=[iq], tag=f"dq1_{tb}")
        dg = S.comp("dve", lambda e, tb=tb: e.tensor_scalar(
            out=diagb, in0=ident, scalar1=iq1[:, tb:tb + 1], scalar2=None,
            op0=ALU.mult),
            waits=[f"iq1_{tb}", "ld_ident"] + opt(f"idr1t_{tb-1}"),
            tag=f"idr1d_{tb}")
        tp0 = S.comp("pe", lambda e, tb=tb: e.matmul(
            pb[5][0:1, (tb % 4) * 128:(tb % 4) * 128 + 128],
            ones[:, 0:1], diagb, start=True, stop=True),
            waits=[dg, "ld_ones"] + opt(f"idr1_{tb-4}"), tag=f"idr1t_{tb}")
        S.comp("dve", lambda e, tb=tb: e.tensor_copy(
            idr1[0:1, tb * 128:(tb + 1) * 128],
            pb[5][0:1, (tb % 4) * 128:(tb % 4) * 128 + 128]), waits=[tp0],
            tag=f"idr1_{tb}")
        w1_ = S.comp("dve", lambda e, tb=tb: e.tensor_scalar(
            out=wb, in0=zb, scalar1=q1_a[:, tb:tb + 1],
            scalar2=q1_b[:, tb:tb + 1], op0=ALU.mult, op1=ALU.add),
            waits=[b_] + opt(f"xq1m_{tb-1}"), tag=f"xq1w_{tb}")
        kcv = S.comp("dve", lambda e: e.tensor_copy(ki1, wb),
                     waits=[w1_], tag=f"xq1k_{tb}")
        dd = S.comp("dve", lambda e: e.tensor_sub(mb_, wb, ki1),
                    waits=[kcv], tag=f"xq1d_{tb}")
        cc = S.comp("dve", lambda e: e.tensor_scalar(
            out=mb_, in0=mb_, scalar1=magp[:, 0:1], scalar2=magn[:, 0:1],
            op0=ALU.add, op1=ALU.add), waits=[dd, "magp", "magn"],
            tag=f"xq1c_{tb}")
        xq = S.comp("dve", lambda e: e.tensor_add(xqb, mb_, ki1),
                    waits=[cc] + opt(f"xq1tp_{tb-1}_{ND-1}"),
                    tag=f"xq1m_{tb}")
        for dc in range(ND):
            bank = pbf[dc % 2]
            tp = S.comp("pe", lambda e, dc=dc, bank=bank: e.matmul(
                bank[:, 0:128], xqb[:, dc * 128:(dc + 1) * 128], identb[:, :],
                start=True, stop=True, is_transpose=True),
                waits=[xq] + (opt(f"xq1ev_{tb}_{dc-2}") if dc >= 2 else
                              opt(f"xq1ev_{tb-1}_{ND-2+dc}", "bar_m")),
                tag=f"xq1tp_{tb}_{dc}")
            S.comp("act", lambda e, dc=dc, tb=tb, bank=bank: e.activation(
                out=xq1t[:, dc * T + tb * 128:dc * T + (tb + 1) * 128],
                in_=bank[:, 0:128], func=AF.Copy), waits=[tp],
                tag=f"xq1ev_{tb}_{dc}")


    F2 = F1
    b1r = sb(F2, "b1rT", 1, FF)
    wq1s = sb(F2, "wq1s", 128, 2 * ND * 512, BF16)
    gb = sb(F2, "gb", 128, 2 * 512)

    S.dma("sp", "cb1", lambda e: e.dma_start(out=b1r, in_=b1_h[:, :]),
          tag="ld_b1", waits=["bar_m"])

    for fs in range(NFS):
        sl = fs % 2
        lw = S.dma("sp", f"wq1l{sl}", lambda e, fs=fs, sl=sl: e.dma_start(
            out=wq1s[:, sl * ND * 512:(sl + 1) * ND * 512],
            in_=wq1t_h[fs, :, :]),
            waits=[f"w1st_{fs}"] + opt(f"mm1_{fs-2}_{NTB-1}"),
            tag=f"wq1l_{fs}")
        for tb in range(NTB):
            bank = pb[tb % 2]
            mm = S.comp("pe", lambda e, fs=fs, sl=sl, tb=tb, bank=bank: [
                e.matmul(bank[:, 0:512],
                         xq1t[:, dc * T + tb * 128:dc * T + (tb + 1) * 128],
                         wq1s[:, sl * ND * 512 + dc * 512:
                              sl * ND * 512 + (dc + 1) * 512],
                         start=(dc == 0), stop=False)
                for dc in range(ND)] + [
                e.matmul(bank[:, 0:512], idr1[0:1, tb * 128:(tb + 1) * 128],
                         b1r[0:1, fs * 512:(fs + 1) * 512],
                         start=False, stop=True)],
                waits=[lw, f"xq1ev_{tb}_{ND-1}", f"idr1_{tb}", "ld_b1"] +
                      opt(f"gev_{fs}_{tb-2}", f"gev_{fs-1}_{NTB+tb-2}"),
                n=ND + 1, tag=f"mm1_{fs}_{tb}")
            gv = gb[:, (tb % 2) * 512:(tb % 2) * 512 + 512]
            ge = S.comp("act", lambda e, tb=tb, bank=bank, gv=gv: e.activation(
                out=gv, in_=bank[:, 0:512], func=AF.Gelu,
                scale=dq1[:, tb:tb + 1]),
                waits=[mm, f"dq1_{tb}"] + opt(f"gst_{fs}_{tb-2}",
                                              f"gst_{fs-1}_{NTB+tb-2}"),
                tag=f"gev_{fs}_{tb}")
            S.comp("dve", lambda e, fs=fs, tb=tb, gv=gv: e.bn_stats(
                gbn[:, (tb * NFS + fs) * 6:(tb * NFS + fs) * 6 + 6], gv),
                waits=[ge], tag=f"gbn_{fs}_{tb}")
            S.comp("dve", lambda e, fs=fs, tb=tb, gv=gv: e.tensor_reduce(
                gmx[:, tb * NFS + fs:tb * NFS + fs + 1], gv, AX.X, ALU.max),
                waits=[ge], tag=f"gmx_{fs}_{tb}")
            gmn_t = S.comp("dve", lambda e, fs=fs, tb=tb, gv=gv: e.tensor_reduce(
                gmn[:, tb * NFS + fs:tb * NFS + fs + 1], gv, AX.X, ALU.min),
                waits=[ge], tag=f"gmn_{fs}_{tb}")
            S.dma("pool", f"gs{tb % 2}", lambda e, fs=fs, tb=tb, gv=gv: e.dma_start(
                out=g_h[tb, :, fs * 512:(fs + 1) * 512], in_=gv),
                waits=[gmn_t], tag=f"gst_{fs}_{tb}")

    S.barrier("bar_f2")
    if STOP_PHASE == "bar_f2":
        return probe_finish(ident[0:128, 0:128])
    F2.close()
    xq1t_st.close()

    ID2 = contextlib.ExitStack()
    idr2 = sb(ID2, "idr2", 1, NTB * 128)
    F3 = contextlib.ExitStack()
    QW = FF // 4
    gr = sb(F3, "gr", 128, QW)
    wb3 = sb(F3, "wb3", 128, QW)
    mb3 = sb(F3, "mb3", 128, QW)
    xq2b = sb(F3, "xq2b", 128, QW, BF16)
    ki3 = sb(F3, "ki3", 128, QW, mybir.dt.int32)
    stg3 = sb(F3, "stg3", 128, NFC * 128, BF16)
    gmx3 = gmx
    gmn3 = gmn
    gbn3 = gbn

    for tb in range(NTB):
        ag = S.comp("dve", lambda e, tb=tb: e.bn_aggr(
            bnag, gbn3[:, tb * NFS * 6:(tb + 1) * NFS * 6]),
            waits=opt(f"xq2w_{tb-1}_3", "bar_f2") + agw(), tag=f"gag_{tb}")
        nm = S.comp("dve", lambda e: e.tensor_scalar_mul(
            sst[:, 0:1], bnag[:, 0:1], -1.0), waits=[ag])
        rs = rstd(ag, sst[:, 1:2])
        mx = S.comp("dve", lambda e, tb=tb: e.tensor_reduce(
            sst[:, 2:3], gmx3[:, tb * NFS:(tb + 1) * NFS], AX.X, ALU.max),
            waits=[ag])
        mn = S.comp("dve", lambda e, tb=tb: e.tensor_reduce(
            sst[:, 3:4], gmn3[:, tb * NFS:(tb + 1) * NFS], AX.X, ALU.min),
            waits=[ag])
        am1 = S.comp("dve", lambda e: e.tensor_add(
            sst[:, 2:3], sst[:, 2:3], sst[:, 0:1]), waits=[mx, nm])
        am2 = S.comp("dve", lambda e: e.scalar_tensor_tensor(
            out=sst[:, 3:4], in0=sst[:, 3:4], scalar=-1.0, in1=sst[:, 0:1],
            op0=ALU.mult, op1=ALU.subtract), waits=[mn, nm])
        am = S.comp("dve", lambda e: e.tensor_max(
            sst[:, 2:3], sst[:, 2:3], sst[:, 3:4]), waits=[am1, am2])
        amr = S.comp("dve", lambda e: e.tensor_scalar(
            out=sst[:, 2:3], in0=sst[:, 2:3], scalar1=sst[:, 1:2], scalar2=EPS,
            op0=ALU.mult, op1=ALU.max), waits=[am, rs])
        rec = S.comp("dve", lambda e: e.reciprocal(sst[:, 4:5], sst[:, 2:3]),
                     waits=[amr])
        sc = S.comp("dve", lambda e: e.tensor_scalar_mul(
            sst[:, 4:5], sst[:, 4:5], 127.0), waits=[rec])
        a_ = S.comp("dve", lambda e, tb=tb: e.tensor_mul(
            q2_a[:, tb:tb + 1], sst[:, 1:2], sst[:, 4:5]), waits=[sc],
            tag=f"q2a_{tb}")
        b_ = S.comp("dve", lambda e, tb=tb: e.tensor_mul(
            q2_b[:, tb:tb + 1], sst[:, 0:1], q2_a[:, tb:tb + 1]),
            waits=[a_], tag=f"q2b_{tb}")
        iq = S.comp("dve", lambda e, tb=tb: e.tensor_mul(
            iq2[:, tb:tb + 1], sst[:, 4:5], ws2v[:, 0:1]), waits=[sc, tw2],
            tag=f"iq2_{tb}")
        S.comp("dve", lambda e, tb=tb: e.reciprocal(
            dq2[:, tb:tb + 1], iq2[:, tb:tb + 1]), waits=[iq], tag=f"dq2_{tb}")
        dg2 = S.comp("dve", lambda e, tb=tb: e.tensor_scalar(
            out=diagb, in0=ident, scalar1=iq2[:, tb:tb + 1], scalar2=None,
            op0=ALU.mult),
            waits=[f"iq2_{tb}"] + opt(f"idr2t_{tb-1}", f"idr1t_{NTB-1}"),
            tag=f"idr2d_{tb}")
        tp0 = S.comp("pe", lambda e, tb=tb: e.matmul(
            pb[5][0:1, (tb % 4) * 128:(tb % 4) * 128 + 128],
            ones[:, 0:1], diagb, start=True, stop=True),
            waits=[dg2] + opt(f"idr2_{tb-4}", f"idr1_{NTB-1}"),
            tag=f"idr2t_{tb}")
        S.comp("dve", lambda e, tb=tb: e.tensor_copy(
            idr2[0:1, tb * 128:(tb + 1) * 128],
            pb[5][0:1, (tb % 4) * 128:(tb % 4) * 128 + 128]), waits=[tp0],
            tag=f"idr2_{tb}")
        evs = []
        for q in range(4):
            lt = S.dma("sp", "grl", lambda e, tb=tb, q=q: e.dma_start(
                out=gr, in_=g_h[tb, :, q * QW:(q + 1) * QW]),
                waits=opt(f"xq2q_{tb}_{q-1}", f"xq2q_{tb-1}_3", "bar_f2"),
                tag=f"grl_{tb}_{q}")
            w2_ = S.comp("dve", lambda e, tb=tb: e.tensor_scalar(
                out=wb3, in0=gr, scalar1=q2_a[:, tb:tb + 1],
                scalar2=q2_b[:, tb:tb + 1], op0=ALU.mult, op1=ALU.add),
                waits=[b_, lt] + opt(f"xq2q_{tb}_{q-1}", f"xq2q_{tb-1}_3"),
                tag=f"xq2w_{tb}_{q}")
            kcv = S.comp("dve", lambda e: e.tensor_copy(ki3, wb3),
                         waits=[w2_], tag=f"xq2k_{tb}_{q}")
            dd = S.comp("dve", lambda e: e.tensor_sub(mb3, wb3, ki3),
                        waits=[kcv], tag=f"xq2d_{tb}_{q}")
            cc = S.comp("dve", lambda e: e.tensor_scalar(
                out=mb3, in0=mb3, scalar1=magp[:, 0:1], scalar2=magn[:, 0:1],
                op0=ALU.add, op1=ALU.add), waits=[dd, "magp", "magn"],
                tag=f"xq2c_{tb}_{q}")
            xq = S.comp("dve", lambda e: e.tensor_add(xq2b, mb3, ki3),
                        waits=[cc] + opt(f"xq2tp_{tb}_{q*16-1}",
                                         f"xq2tp_{tb-1}_{NFC-1}"),
                        tag=f"xq2q_{tb}_{q}")
            for fl in range(16):
                fc = q * 16 + fl
                bank = pbf[fc % 2]
                tp = S.comp("pe", lambda e, fl=fl, fc=fc, bank=bank: e.matmul(
                    bank[:, 0:128], xq2b[:, fl * 128:(fl + 1) * 128],
                    identb[:, :], start=True, stop=True, is_transpose=True),
                    waits=[xq] + (opt(f"xq2ev_{tb}_{fc-2}") if fc >= 2 else
                                  opt(f"xq2ev_{tb-1}_{NFC-2+fc}", "bar_f2")),
                    tag=f"xq2tp_{tb}_{fc}")
                ev = S.comp("act", lambda e, fc=fc, bank=bank: e.activation(
                    out=stg3[:, fc * 128:(fc + 1) * 128], in_=bank[:, 0:128],
                    func=AF.Copy),
                    waits=[tp] + opt(f"xq2st_{tb-1}" if fc == 0 else "~"),
                    tag=f"xq2ev_{tb}_{fc}")
                if fl == 15:
                    evs.append(ev)
        S.dma("pool", "xq2s", lambda e, tb=tb: e.dma_start(
            out=xq2t_h[tb, :, :], in_=stg3), waits=evs, tag=f"xq2st_{tb}")

    F4 = F3
    b2r = sb(F4, "b2rT", 1, D)
    wq2s = sb(F4, "wq2s", 128, NFC * 512, BF16)
    xq2r = sb(F4, "xq2r", 128, 2 * NFC * 128, BF16)
    y2b = sb(F4, "y2b", 128, 512)

    S.dma("sp", "cb2", lambda e: e.dma_start(out=b2r, in_=b2_h[:, :]),
          tag="ld_b2", waits=["bar_f2"])

    for dq in range(NDQ):
        lws = []
        for j in range(4):
            si = dq * 4 + j
            lws.append(S.dma("sp", f"wq2l{j}", lambda e, si=si, j=j: e.dma_start(
                out=wq2s.rearrange("p (a b) -> p a b", b=512)[:, :, j * 128:(j + 1) * 128],
                in_=wq2t_h[si, :, :].rearrange("p (a b) -> p a b", b=128)),
                waits=opt(f"mm2_{dq-1}_{NTB-1}", "bar_f2"),
                tag=f"wq2l_{dq}_{j}"))
        for tb in range(NTB):
            sl = tb % 2
            lx = S.dma("sp", f"xq2l{sl}", lambda e, tb=tb, sl=sl: e.dma_start(
                out=xq2r[:, sl * NFC * 128:(sl + 1) * NFC * 128],
                in_=xq2t_h[tb, :, :]),
                waits=[f"xq2st_{tb}"] +
                      opt(f"mm2_{dq}_{tb-2}", f"mm2_{dq-1}_{NTB+tb-2}"),
                tag=f"xq2l_{dq}_{tb}")
            bank = pb[2 + tb % 2]
            mm = S.comp("pe", lambda e, dq=dq, tb=tb, sl=sl, bank=bank: [
                e.matmul(bank[:, 0:512],
                         xq2r[:, sl * NFC * 128 + fc * 128:
                              sl * NFC * 128 + (fc + 1) * 128],
                         wq2s[:, fc * 512:(fc + 1) * 512],
                         start=(fc == 0), stop=False)
                for fc in range(NFC)] + [
                e.matmul(bank[:, 0:512], idr2[0:1, tb * 128:(tb + 1) * 128],
                         b2r[0:1, dq * 512:(dq + 1) * 512],
                         start=False, stop=True)],
                waits=lws + [lx, f"idr2_{tb}", "ld_b2"] +
                      opt(f"y2ev_{dq}_{tb-2}", f"y2ev_{dq-1}_{NTB+tb-2}"),
                n=NFC + 1, tag=f"mm2_{dq}_{tb}")
            ye = S.comp("act", lambda e, tb=tb, bank=bank: e.activation(
                out=y2b, in_=bank[:, 0:512], func=AF.Copy,
                scale=dq2[:, tb:tb + 1]),
                waits=[mm, f"dq2_{tb}"] + opt(f"y2st_{dq}_{tb-1}",
                                              f"y2st_{dq-1}_{NTB-1}"),
                tag=f"y2ev_{dq}_{tb}")
            S.dma("pool", "y2s", lambda e, dq=dq, tb=tb: e.dma_start(
                out=y2_h[tb, :, dq * 512:(dq + 1) * 512], in_=y2b),
                waits=[ye], tag=f"y2st_{dq}_{tb}")


    F5 = F4
    y2r = sb(F5, "y2r", 128, D)
    outb = sb(F5, "outb", 128, D)

    for tb in range(NTB):
        lt = S.dma("sp", "y2l", lambda e, tb=tb: e.dma_start(
            out=y2r, in_=y2_h[tb, :, :]), waits=[f"y2st_{NDQ-1}_{tb}"] + opt(f"oaf_{tb-1}"),
            tag=f"y2l_{tb}")
        lr = S.dma("sp", "r1l2", lambda e, tb=tb: e.dma_start(
            out=outb, in_=r1_h[tb, :, :]),
            waits=opt(f"ost_{tb-1}"), tag=f"r1l2_{tb}")
        bs = S.comp("dve", lambda e: [
            e.bn_stats(bnst[:, 6 * i:6 * (i + 1)], y2r[:, 512 * i:512 * (i + 1)])
            for i in range(4)], waits=[lt] + opt(f"oaf_{tb-1}"), n=4)
        ag = S.comp("dve", lambda e: e.bn_aggr(bnag, bnst[:, 0:24]), waits=[bs] + agw())
        nm = S.comp("dve", lambda e: e.tensor_scalar_mul(
            sst[:, 0:1], bnag[:, 0:1], -1.0), waits=[ag])
        rs = rstd(ag, sst[:, 1:2])
        bb_ = S.comp("dve", lambda e: e.tensor_mul(
            sst[:, 2:3], sst[:, 0:1], sst[:, 1:2]), waits=[nm, rs])
        af0 = S.comp("dve", lambda e: e.tensor_scalar(
            out=y2r, in0=y2r, scalar1=sst[:, 1:2], scalar2=sst[:, 2:3],
            op0=ALU.mult, op1=ALU.add), waits=[bb_], tag=f"oaa_{tb}")
        af = S.comp("dve", lambda e: e.tensor_add(outb, y2r, outb),
                    waits=[af0, lr], tag=f"oaf_{tb}")
        S.dma("pool", "outs", lambda e, tb=tb: e.dma_start(
            out=out_h[tb * 128:(tb + 1) * 128, :], in_=outb),
            waits=[af], tag=f"ost_{tb}")

    S.comp("pool", lambda e: e.engine_nop(), waits=[f"ost_{NTB-1}"])

    S.emit(nc, stack)
    F5.close()
    ID2.close()
    GS.close()
    stack.close()
    return nc


def _host_prep(L0_attn_A_log, L0_attn_D):
    import ml_dtypes
    alog = np.asarray(L0_attn_A_log, dtype=np.float32)
    dvec = np.asarray(L0_attn_D, dtype=np.float32)
    app = np.ascontiguousarray(
        alog.reshape(ND, 128, NST).transpose(1, 0, 2).reshape(128, ND * NST))
    dpp = np.ascontiguousarray(dvec.reshape(ND, 128).T)
    ident = np.eye(128, dtype=np.float32)
    identb = np.eye(128, dtype=np.float32).astype(ml_dtypes.bfloat16)
    ones = np.ones((128, 512), dtype=np.float32)
    repl = np.zeros((32, 32 * 128), dtype=np.float32)
    for k in range(32):
        repl[k, k * 128:(k + 1) * 128] = 1.0
    return app, dpp, ident, identb, ones, repl


def kernel(L0_attn_x, L0_attn_A_log, L0_attn_D, L0_attn_W_xproj,
           L0_ffn_w1, L0_ffn_b1, L0_ffn_w2, L0_ffn_b2):
    x = np.ascontiguousarray(np.asarray(L0_attn_x, dtype=np.float32))
    app, dpp, ident, identb, ones, repl = _host_prep(L0_attn_A_log, L0_attn_D)
    shared = {
        "wxp": np.ascontiguousarray(np.asarray(L0_attn_W_xproj, np.float32)),
        "app": app, "dpp": dpp,
        "w1": np.ascontiguousarray(np.asarray(L0_ffn_w1, np.float32)),
        "b1r": np.ascontiguousarray(np.asarray(L0_ffn_b1, np.float32)[None, :]),
        "w2": np.ascontiguousarray(np.asarray(L0_ffn_w2, np.float32)),
        "b2r": np.ascontiguousarray(np.asarray(L0_ffn_b2, np.float32)[None, :]),
        "ident": ident, "identb": identb, "ones": ones, "repl": repl,
    }
    nc = build_nc()
    in_maps = [dict(shared, x=np.ascontiguousarray(x[b])) for b in range(8)]
    res = run_bass_kernel_spmd(nc, in_maps, core_ids=list(range(8)))
    out = np.stack([res.results[b]["out"] for b in range(8)], axis=0)
    return out.astype(np.float32)
